# version 11
# baseline (speedup 1.0000x reference)
import numpy as np
import ml_dtypes
from contextlib import ExitStack

import concourse.bass as bass
import concourse.mybir as mybir
from concourse.bass_utils import run_bass_kernel_spmd

F32 = mybir.dt.float32
BF16 = mybir.dt.bfloat16
I32 = mybir.dt.int32
ALU = mybir.AluOpType
AF = mybir.ActivationFunctionType
AX = mybir.AxisListType

P = 128
D = 1024
DC = D // P
F = 4096
FC = F // P
FCH = FC // 2
T_OWN = 2048
NT_OWN = T_OWN // P
T_FOR = 6144
NT_ST = (T_OWN + T_FOR) // P
TB = 512
NTB_ALL = (T_OWN + T_FOR) // TB
NTB_OWN = T_OWN // TB
MAGIC = 12582912.0
EPS_LN = 1e-5
EPS_Q = 1e-8
NCORE = 8
NEX = 10


class Prog:
    def __init__(self, nc, t, sems, taps=()):
        self.nc, self.t, self.sems = nc, t, sems
        self.taps = set(taps)

    def walk(self, engine, eng):
        self.engine, self.eng = engine, eng
        self.regn = 0
        self.poll_reg = None
        self.counts = {k: 0 for k in self.sems}
        self.waited = {k: 0 for k in self.sems}
        self.g = {}
        self.bank_guard = [[] for _ in range(4)]
        self.ps_ctr = 0
        self.tr_ctr = 0
        self.wq_pc = 0
        self.program()

    def I(self, engine, emit, waits=(), inc=None, inc_n=1):
        if engine in ("vector", "scalar", "gpsimd"):
            waits = list(waits) + [(engine, self.counts[engine])]
        if inc is not None:
            self.counts[inc] += inc_n
        if self.engine == engine:
            for ch, v in waits:
                if v > self.waited[ch]:
                    self.eng.wait_ge(self.sems[ch], v)
                    self.waited[ch] = v
            ins = emit(self.eng)
            if inc is not None:
                ins.then_inc(self.sems[inc], inc_n)
        return self.counts[inc] if inc is not None else None

    def W(self, engine, waits):
        if self.engine == engine:
            for ch, v in waits:
                if v > self.waited[ch]:
                    self.eng.wait_ge(self.sems[ch], v)
                    self.waited[ch] = v

    def dma(self, q, sem, out, in_, waits=(), **kw):
        return self.I(q, lambda e: e.dma_start(out=out, in_=in_, **kw),
                      waits=waits, inc=sem, inc_n=16)

    def mmgroup(self, kmms, waits=(), mm_waits=None):
        bank = self.ps_ctr % 4
        self.ps_ctr += 1
        w = list(waits) + self.bank_guard[bank]
        self.bank_guard[bank] = []
        mark = None
        n = len(kmms)
        for k, (lh, rh) in enumerate(kmms):
            w_k = list(w) if k == 0 else []
            if mm_waits is not None:
                w_k += list(mm_waits[k])
            mark = self.I("tensor", lambda e, lh=lh, rh=rh, k=k, n=n, bank=bank:
                          e.matmul(self.t["ps_mm"][bank][:, 0:TB], lh, rh,
                                   start=(k == 0), stop=(k == n - 1)),
                          waits=w_k,
                          inc="tensor" if k == n - 1 else None)
        return bank, mark

    def bank_done(self, bank, chan, mark):
        self.bank_guard[bank].append((chan, mark))

    def ph_warm(self):
        t = self.t
        self.I("gpsimd", lambda e: e.memset(t["init_c"][:, 0:1], 0.0),
               waits=[("bC", self.c_consts)], inc="gpsimd")
        self.I("gpsimd", lambda e: e.memset(t["pollf"][0:1, 0:2], 0.0),
               inc="gpsimd")
        self.W("gpsimd", [("gpsimd", self.counts["gpsimd"])])
        self.counts["q_poll"] += 16
        if self.engine == "gpsimd":
            e = self.eng
            self.regn += 1
            with e.register("wr%d" % self.regn) as par:
                e.reg_load(par, t["parity_i"][0:1, 0:1])
                for s in range(2):
                    with e.If_eq(par, s):
                        e.dma_start(out=t["pollf"][0:1, 0:1],
                                    in_=t["shm_flag"][0:1, s:s + 1]
                                    ).then_inc(self.sems["q_poll"], 16)
            self.poll_reg = e.alloc_register("pollcnt")
            e.reg_mov(self.poll_reg, 16)
        self.W("gpsimd", [("q_poll", self.counts["q_poll"])])

    def send(self, ex, payload_ap, rows):
        t = self.t
        shd = t["shm_pay"][ex]
        self.counts["q_gpsimd"] += 16
        pay_done = self.counts["q_gpsimd"]
        if self.engine == "gpsimd":
            e = self.eng
            self.regn += 1
            with e.register("par%d" % self.regn) as par:
                e.reg_load(par, t["parity_i"][0:1, 0:1])
                for s in range(2):
                    with e.If_eq(par, s):
                        e.dma_start(out=shd[s * rows:(s + 1) * rows, :],
                                    in_=payload_ap
                                    ).then_inc(self.sems["q_gpsimd"], 16)
            e.wait_ge(self.sems["q_gpsimd"], pay_done)
            self.waited["q_gpsimd"] = pay_done
        self.counts["q_gpsimd"] += 16
        if self.engine == "gpsimd":
            e = self.eng
            self.regn += 1
            with e.register("parf%d" % self.regn) as par:
                e.reg_load(par, t["parity_i"][0:1, 0:1])
                for s in range(2):
                    with e.If_eq(par, s):
                        e.dma_start(
                            out=t["shm_flag"][0:1, 2 * ex + s:2 * ex + s + 1],
                            in_=t["nonces_i"][0:1, 2 * ex + s:2 * ex + s + 1]
                            ).then_inc(self.sems["q_gpsimd"], 16)

    def recv(self, ex, recv_sb, src_ap):
        t = self.t
        self.W("gpsimd", [("q_gpsimd", self.counts["q_gpsimd"])])
        if self.engine == "gpsimd":
            e = self.eng
            if getattr(self, "poll_reg", None) is None:
                self.poll_reg = e.alloc_register("pollcnt")
                e.reg_mov(self.poll_reg, 0)
            cnt = self.poll_reg
            self.regn += 1
            rn = self.regn
            with e.register("w%d" % rn) as w, \
                 e.register("x%d" % rn) as x, e.register("m%d" % rn) as miss, \
                 e.register("ne%d" % rn) as ne:

                def cond():
                    e.reg_add(cnt, cnt, 16)
                    e.dma_start(out=t["pollf"][0:1, 0:2],
                                in_=t["shm_flag"][0:1, 2 * ex:2 * ex + 2]
                                ).then_inc(self.sems["q_poll"], 16)
                    e.wait_ge(self.sems["q_poll"], cnt)
                    e.reg_mov(miss, 0)
                    for s in range(2):
                        e.reg_load(w, t["pollf"][0:1, s:s + 1])
                        e.reg_load(x, t["nonces_i"][0:1,
                                                    2 * ex + s:2 * ex + s + 1])
                        e.reg_alu(ne, w, x, ALU.not_equal)
                        e.reg_add(miss, miss, ne)
                    return miss
                with e.While(cond):
                    pass
        return self.dma("gpsimd", "q_gpsimd", recv_sb, src_ap)

    def program(self):
        self.ph_consts()
        self.ph_warm()
        self.ph_stats()
        self.ph_s1()
        self.ph_attn()
        self.ph_ffn1()
        self.ph_s2()
        self.ph_final()
        self.W("sync", [("ou0", self.counts["ou0"]),
                        ("ou1", self.counts["ou1"])])

    def ph_consts(self):
        t = self.t
        for name in ("eye16", "eye32", "ones1", "onesb", "b1t", "b2row", "nwt",
                     "cmaskv", "consv", "nonces_i", "parity_i"):
            self.dma("sync", "bC", t[name][:], t["in_" + name])
        self.c_consts = self.counts["bC"]
        self.v_negnw = self.I("vector", lambda e: e.tensor_scalar(
            t["negnw"][:], t["nwt"][:], -1.0, None, ALU.mult),
            waits=[("bC", self.c_consts)], inc="vector")

    def ph_cast(self):
        t = self.t
        for wi, w in enumerate(("wi", "wf", "wg", "wo")):
            for kk in range(2 * DC):
                k = wi * 2 * DC + kk
                b = k % 2
                r, c = kk // 2, kk % 2
                ld = self.dma("scalar", "wl%d" % b, t["hrt"][b][:, 0:TB],
                              t["in_" + w][r * P:(r + 1) * P,
                                           c * TB:(c + 1) * TB],
                              waits=[("vector", self.g.get(("cst", b), 0))])
                cp = self.I("vector", lambda e, b=b: e.tensor_copy(
                    t["o16"][b][:, 0:TB], t["hrt"][b][:, 0:TB]),
                    waits=[("wl%d" % b, ld),
                           ("ws%d" % b, self.g.get(("wqst", b), 0))],
                    inc="vector")
                self.g[("cst", b)] = cp
                st = self.dma("scalar", "ws%d" % b,
                              t["dsc_" + w][r * P:(r + 1) * P,
                                            c * TB:(c + 1) * TB],
                              t["o16"][b][:, 0:TB], waits=[("vector", cp)])
                self.g[("wqst", b)] = st
        self.c_cast = (self.counts["ws0"], self.counts["ws1"])
        self.v_castw = self.counts["vector"]

    def ph_wt(self):
        t = self.t
        for w, buf in (("wi", "wti"), ("wf", "wtf"), ("wg", "wtg")):
            for k in range(DC):
                self.dma("scalar", "bWt", t[buf][k][:],
                         t["dsc_" + w][0:D, k * P:(k + 1) * P],
                         waits=[("ws0", self.c_cast[0]),
                                ("ws1", self.c_cast[1])], transpose=True)
        self.c_wt = self.counts["bWt"]

    def _quant_w(self, src, dst, sw_tile, k0, k1, width, cfg="scan"):
        t = self.t
        hw = 1024 if cfg == "scan" else 512
        ldx = t["wldx"] if cfg == "scan" else t["wldx_f"]
        wy1 = t["wld1"] if cfg == "scan" else t["wld1_f"]
        w16 = t["wq16"] if cfg == "scan" else t["wq16_f"]
        halves = width // hw

        def wload(h, extra=()):
            pc = self.wq_pc
            self.wq_pc += 1
            b2 = pc % 2
            gsl = self.g.get(("wqs", cfg, b2), {})
            ld = self.dma("scalar", "wl%d" % b2, ldx[b2][:, 0:hw],
                          src[k * P:(k + 1) * P, h * hw:(h + 1) * hw],
                          waits=[("scalar",
                                  gsl.get("s", self.g.get(("wq_act", cfg), 0))),
                                 ("vector",
                                  gsl.get("v", self.g.get(("wq_mx", cfg), 0)))] +
                          list(extra) + list(self.g.get(("wq_ld0", cfg), ())))
            return b2, ld

        for k in range(k0, k1):
            b = k % 2
            for h in range(halves):
                b2, ld = wload(h)
                vh = self.I("vector", lambda e, h=h, hw=hw, b2=b2, ldx=ldx:
                            e.reduce_max(t["wred"][:, h:h + 1],
                                         ldx[b2][:, 0:hw], AX.X,
                                         apply_absolute_value=True),
                            waits=[("wl%d" % b2, ld)], inc="vector")
                g = dict(self.g.get(("wqs", cfg, b2), {}))
                g["v"] = vh
                self.g[("wqs", cfg, b2)] = g
            self.g[("wq_mx", cfg)] = vh
            for h in range(1, halves):
                self.I("vector", lambda e, h=h: e.tensor_tensor(
                    t["wred"][:, 0:1], t["wred"][:, 0:1], t["wred"][:, h:h + 1],
                    ALU.max), inc="vector")
            self.I("vector", lambda e, k=k: e.tensor_scalar(
                sw_tile[:, k:k + 1], t["wred"][:, 0:1], EPS_Q, 1.0 / 127.0,
                ALU.max, ALU.mult), inc="vector")
            vr = self.I("vector", lambda e, k=k: e.reciprocal(
                t["wred"][:, 8:9], sw_tile[:, k:k + 1]), inc="vector")
            for h in range(halves):
                if halves > 1:
                    b2, ld = wload(h, extra=[("vector", vr)])
                else:
                    b2 = (self.wq_pc - 1) % 2
                    ld = self.counts["wl%d" % b2]
                sa = self.I("scalar", lambda e, hw=hw, b2=b2, ldx=ldx,
                            wy1=wy1: e.activation(
                    wy1[:, 0:hw], ldx[b2][:, 0:hw], AF.Identity,
                    bias=t["consv"][:, 0:1], scale=t["wred"][:, 8:9]),
                    waits=[("vector", vr), ("wl%d" % b2, ld),
                           ("vector", self.g.get(("wq_g2", cfg), 0))],
                    inc="scalar")
                g = dict(self.g.get(("wqs", cfg, b2), {}))
                g["s"] = sa
                self.g[("wqs", cfg, b2)] = g
                g2 = self.I("vector", lambda e, b=b, hw=hw, wy1=wy1, w16=w16:
                            e.tensor_scalar(
                    w16[b][:, 0:hw], wy1[:, 0:hw], MAGIC, None,
                    ALU.subtract),
                    waits=[("scalar", sa),
                           ("ws%d" % b, self.g.get(("wqst", cfg, b), 0))] +
                    list(self.g.get(("wq_g2x", cfg), ())),
                    inc="vector")
                self.g[("wq_g2", cfg)] = g2
                st = self.dma("scalar", "ws%d" % b,
                              dst[k * P:(k + 1) * P, h * hw:(h + 1) * hw],
                              w16[b][:, 0:hw], waits=[("vector", g2)])
                self.g[("wqst", cfg, b)] = st
            self.g[("wq_act", cfg)] = sa

    def ph_stats(self):
        t = self.t
        for i in range(NT_ST):
            b = i % 2
            g = self.g.get(("st", b), {})
            tile = t["xtile"][b]
            ld = self.dma("sync", "xt%d" % b, tile[:],
                          t["in_xst"][i * P:(i + 1) * P, :],
                          waits=[("scalar", g.get("act", 0)),
                                 ("vector", g.get("dve", 0))])
            sa = self.I("scalar", lambda e, tile=tile, i=i: e.activation(
                t["scr_a"][:], tile[:], AF.Square,
                accum_out=t["sqc"][:, i:i + 1]), waits=[("xt%d" % b, ld)],
                inc="scalar")
            self.I("vector", lambda e, tile=tile, i=i: e.tensor_scalar(
                t["scr_v"][:], tile[:], 1.0, 0.0, ALU.mult, ALU.add,
                accum_out=t["sumc"][:, i:i + 1]), waits=[("xt%d" % b, ld)],
                inc="vector")
            v2 = self.I("vector", lambda e, tile=tile, i=i: e.pool_max(
                t["xmaxc"][:, i:i + 1], tile[:]), inc="vector")
            self.g[("st", b)] = {"act": sa, "dve": v2}
        c_act, c_dve = self.counts["scalar"], self.counts["vector"]

        cc = 1.0 / D
        self.I("vector", lambda e: e.tensor_scalar(
            t["m_t"][:], t["sumc"][:], cc, None, ALU.mult),
            waits=[("vector", c_dve), ("scalar", c_act)], inc="vector")
        self.I("vector", lambda e: e.tensor_tensor(
            t["var_t"][:], t["m_t"][:], t["m_t"][:], ALU.mult), inc="vector")
        self.I("vector", lambda e: e.scalar_tensor_tensor(
            t["var_t"][:], t["sqc"][:], cc, t["var_t"][:], ALU.mult,
            ALU.subtract), inc="vector")
        vv = self.I("vector", lambda e: e.tensor_copy(
            t["sd_t"][:], t["var_t"][:]), inc="vector")
        sa = self.I("scalar", lambda e: e.activation(
            t["sd_t"][:], t["sd_t"][:], AF.Sqrt, bias=t["consv"][:, 1:2]),
            waits=[("vector", vv)], inc="scalar")
        self.I("vector", lambda e: e.reciprocal(t["rstd_t"][:], t["sd_t"][:]),
               waits=[("scalar", sa)], inc="vector")
        self.I("vector", lambda e: e.scalar_tensor_tensor(
            t["xbias_t"][:], t["m_t"][:], -1.0, t["rstd_t"][:], ALU.mult,
            ALU.mult), inc="vector")
        self.I("vector", lambda e: e.tensor_tensor(
            t["cand_t"][:], t["xmaxc"][:], t["m_t"][:], ALU.subtract),
            inc="vector")
        self.I("vector", lambda e: e.tensor_tensor(
            t["cand_t"][:], t["cand_t"][:], t["rstd_t"][:], ALU.mult),
            inc="vector")
        vred = self.I("vector", lambda e: e.pool_max(
            t["red1"][:, 0:1], t["cand_t"][:]), inc="vector")
        self.v_stats = self.counts["vector"]
        self.g_cand = self.I("gpsimd", lambda e: e.reduce_max(
            t["scal"][0:1, 0:1], t["red1"][:, 0:1], AX.C),
            waits=[("vector", vred)], inc="gpsimd")

    def ph_s1(self):
        t = self.t
        self.W("gpsimd", [("gpsimd", self.g_cand)])
        self.send(0, t["scal"][0:1, 0:1], 1)
        rd = self.recv(0, t["exch2"][0:2, 0:1], t["shm_pay"][0])
        self.g1_s1 = self.I("gpsimd", lambda e: e.reduce_max(
            t["scal"][0:1, 1:2], t["exch2"][0:2, 0:1], AX.C),
            waits=[("q_gpsimd", rd)], inc="gpsimd")

    def ph_s1b(self):
        t = self.t
        self.I("vector", lambda e: e.tensor_scalar(
            t["scal"][0:1, 2:3], t["scal"][0:1, 1:2], EPS_Q, 1.0 / 255.0,
            ALU.max, ALU.mult), waits=[("gpsimd", self.g1_s1)],
            inc="vector")
        v2 = self.I("vector", lambda e: e.reciprocal(
            t["scal"][0:1, 3:4], t["scal"][0:1, 2:3]), inc="vector")
        mm = self.I("tensor", lambda e: e.matmul(
            t["ps_bc"][0][:, 0:2], t["ones1"][0:1, :], t["scal"][0:1, 2:4],
            start=True, stop=True),
            waits=[("vector", v2), ("bC", self.c_consts)], inc="tensor")
        ve = self.I("vector", lambda e: e.tensor_copy(
            t["s1v"][:, 0:2], t["ps_bc"][0][:, 0:2]),
            waits=[("tensor", mm)], inc="vector")
        self.I("vector", lambda e: e.tensor_scalar(
            t["aq_t"][:], t["rstd_t"][:], t["s1v"][:, 1:2], None, ALU.mult),
            waits=[("vector", ve)], inc="vector")
        self.v_aq = self.I("vector", lambda e: e.tensor_scalar(
            t["bq_t"][:], t["xbias_t"][:], t["s1v"][:, 1:2], MAGIC, ALU.mult,
            ALU.add), inc="vector")
        self.I("vector", lambda e: e.tensor_scalar(
            t["s1sw"][:], t["sw1"][:], t["s1v"][:, 0:1], None, ALU.mult),
            inc="vector")
        self.I("vector", lambda e: e.reciprocal(t["b1q"][:], t["s1sw"][:]),
               inc="vector")
        self.I("vector", lambda e: e.tensor_tensor(
            t["b1q"][:], t["b1t"][:], t["b1q"][:], ALU.mult), inc="vector")
        self.I("vector", lambda e: e.tensor_scalar(
            t["b1q"][:], t["b1q"][:], MAGIC, MAGIC, ALU.add, ALU.subtract),
            inc="vector")
        self.v_b1q = self.I("vector", lambda e: e.tensor_tensor(
            t["b1q"][:], t["b1q"][:], t["s1sw"][:], ALU.mult), inc="vector")

    def _xq_tiles(self, i0, n, dest, act_t, bias_t, relu0, vwait=0,
                  sa_extra=(), ld_extra=()):
        t = self.t
        marks = []
        for j in range(n):
            i = i0 + j
            b = j % 2
            g = self.g.get(("xt", b), {})
            gst = self.g.get(("st", b), {})
            ld = self.dma("sync", "xt%d" % b, t["xtile"][b][:],
                          t["in_xst"][i * P:(i + 1) * P, :],
                          waits=[("scalar", g.get("act", gst.get("act", 0))),
                                 ("vector", g.get("dve", gst.get("dve", 0)))] +
                          list(ld_extra))
            sa = self.I("scalar", lambda e, i=i, b=b, act_t=act_t, bias_t=bias_t:
                        e.activation(t["y32"][b][:], t["xtile"][b][:],
                                     AF.Identity, bias=bias_t[:, i:i + 1],
                                     scale=act_t[:, i:i + 1]),
                        waits=[("xt%d" % b, ld), ("vector", vwait),
                               ("vector", g.get("q", 0))] + list(sa_extra),
                        inc="scalar")
            if relu0:
                vq = self.I("vector", lambda e, b=b: e.tensor_scalar(
                    t["y16"][b][:], t["y32"][b][:], MAGIC, 0.0, ALU.subtract,
                    ALU.max), waits=[("scalar", sa), ("tensor", g.get("tr", 0))],
                    inc="vector")
            else:
                vq = self.I("vector", lambda e, b=b: e.tensor_copy(
                    t["y16"][b][:], t["y32"][b][:]),
                    waits=[("scalar", sa), ("tensor", g.get("tr", 0))],
                    inc="vector")
            self.g[("xt", b)] = {"act": sa, "dve": vq, "q": vq}
            for k in range(DC):
                th = self.tr_ctr % 2
                self.tr_ctr += 1
                mm = self.I("tensor", lambda e, b=b, k=k, th=th: e.transpose(
                    t["ps_tr"][th][:, 0:P],
                    t["y16"][b][:, k * P:(k + 1) * P], t["eye16"][:]),
                    waits=[("vector", vq), ("bC", self.c_consts),
                           ("vector", self.g.get(("tr_ev", th), 0))],
                    inc="tensor")
                if k == DC - 1:
                    self.g[("xt", b)]["tr"] = mm
                ve = self.I("vector", lambda e, dest=dest, j=j, k=k, th=th:
                            e.tensor_copy(dest[k][:, j * P:(j + 1) * P],
                                          t["ps_tr"][th][:, 0:P]),
                    waits=[("tensor", mm)], inc="vector")
                self.g[("tr_ev", th)] = ve
            marks.append(self.counts["vector"])
        return marks

    def ph_attn(self):
        t = self.t
        self.ph_cast()
        self.ph_wt()
        xnb_marks = self._xq_tiles(0, NT_OWN, t["xnb"], t["rstd_t"],
                                   t["xbias_t"], relu0=False,
                                   vwait=self.v_stats)
        v_xnb = xnb_marks[-1]
        self.v_xnb = v_xnb
        self.g[("wq_act", "scan")] = self.counts["scalar"]
        self.g[("wq_mx", "scan")] = self.counts["vector"]
        self.I("vector", lambda e: e.memset(t["sumh2"][0:1, :], 0.0),
               inc="vector")
        fbufs = [t["fbuf"], t["fbuf2"]]
        ibufs = [t["ibuf"], t["ibuf2"]]
        hbs = [t["hrot"], t["hrot2"]]

        def h2_block(j):
            hb = hbs[j % 2]
            for blk in range(NTB_OWN):
                sa = self.I("scalar", lambda e, hb=hb, blk=blk: e.activation(
                    t["h2t"][:, 0:TB], hb[:, blk * TB:(blk + 1) * TB],
                    AF.Square), waits=[("vector", self.g[("scan2", j)]),
                                       ("tensor", self.g.get("h2_mm", 0))],
                    inc="scalar")
                self.g[("h2_act", j % 2)] = sa
                self.g["h2_act"] = sa
                mm = self.I("tensor", lambda e: e.matmul(
                    t["ps_row"][0:1, 0:TB], t["onesb"][:, 0:1],
                    t["h2t"][:, 0:TB], start=True, stop=True),
                    waits=[("scalar", sa), ("bC", self.c_consts),
                           ("vector", self.g.get("h2_add", 0))],
                    inc="tensor")
                self.g["h2_mm"] = mm
                self.g["h2_add"] = self.I(
                    "vector", lambda e, blk=blk: e.tensor_tensor(
                        t["sumh2"][0:1, blk * TB:(blk + 1) * TB],
                        t["sumh2"][0:1, blk * TB:(blk + 1) * TB],
                        t["ps_row"][0:1, 0:TB], ALU.add),
                    waits=[("tensor", mm)], inc="vector")

        for m in range(DC):
            pb = m % 2
            fb, ib = fbufs[pb], ibufs[pb]
            for tb in range(NTB_OWN):
                kf = [(t["wtf"][k][:, m * P:(m + 1) * P],
                       t["xnb"][k][:, tb * TB:(tb + 1) * TB]) for k in range(DC)]
                bank, mmf = self.mmgroup(kf, waits=[
                    ("vector", xnb_marks[4 * tb + 3]), ("bWt", self.c_wt)])
                sf = self.I("scalar", lambda e, tb=tb, bank=bank, fb=fb:
                    e.activation(fb[:, tb * TB:(tb + 1) * TB],
                                 t["ps_mm"][bank][:, 0:TB], AF.Sigmoid),
                    waits=[("tensor", mmf),
                           ("vector", self.g.get(("scan2", m - 2), 0))],
                    inc="scalar")
                self.bank_done(bank, "scalar", sf)
                ki = [(t["wti"][k][:, m * P:(m + 1) * P],
                       t["xnb"][k][:, tb * TB:(tb + 1) * TB]) for k in range(DC)]
                bank, mmi = self.mmgroup(ki)
                si = self.I("scalar", lambda e, bank=bank: e.activation(
                    t["sib"][:, 0:TB], t["ps_mm"][bank][:, 0:TB], AF.Silu),
                    waits=[("tensor", mmi),
                           ("vector", self.g.get("inpn", 0))], inc="scalar")
                self.bank_done(bank, "scalar", si)
                self.g["inpn"] = self.I(
                    "vector", lambda e, tb=tb, fb=fb, ib=ib:
                    e.scalar_tensor_tensor(
                        ib[:, tb * TB:(tb + 1) * TB],
                        fb[:, tb * TB:(tb + 1) * TB], 1.0,
                        t["sib"][:, 0:TB], ALU.subtract, ALU.mult),
                    waits=[("scalar", sf), ("scalar", si)], inc="vector")
                kg = [(t["wtg"][k][:, m * P:(m + 1) * P],
                       t["xnb"][k][:, tb * TB:(tb + 1) * TB]) for k in range(DC)]
                bank, mmg = self.mmgroup(kg)
                b = tb % 2
                sg = self.I("scalar", lambda e, b=b, bank=bank: e.activation(
                    t["gs16"][b][:, 0:TB], t["ps_mm"][bank][:, 0:TB], AF.Silu),
                    waits=[("tensor", mmg),
                           ("sg%d" % b, self.g.get(("gs_st", b), 0))],
                    inc="scalar")
                self.bank_done(bank, "scalar", sg)
                st = self.dma("sync", "sg%d" % b,
                              t["dsc_gs"][m][:, tb * TB:(tb + 1) * TB],
                              t["gs16"][b][:, 0:TB], waits=[("scalar", sg)])
                self.g[("gs_st", b)] = st
                self.g["sg_last"] = sg
            if m > 0:
                h2_block(m - 1)
            hb = hbs[pb]
            v1 = self.I("vector", lambda e, hb=hb, fb=fb, ib=ib:
                e.tensor_tensor_scan(
                    hb[:], fb[:], ib[:], 0.0, ALU.mult, ALU.add),
                waits=[("vector", self.g["inpn"]),
                       ("bAh%d" % pb, self.g.get(("h_st", pb), 0)),
                       ("q_gpsimd", self.g.get("carry_rd", 0)),
                       ("scalar", self.g.get(("h2_act", pb), 0))],
                inc="vector")
            self.W("gpsimd", [("vector", v1),
                              ("vector", self.g.get("init_tt", 0))])
            self.send(1 + m, hb[:, T_OWN - 1:T_OWN], P)
            rd = self.recv(1 + m, t["recv_c"][:, 0:1],
                           t["shm_pay"][1 + m][0:P, :])
            self.g["carry_rd"] = rd
            self.g["init_tt"] = self.I("vector", lambda e: e.tensor_tensor(
                t["init_c"][:, 0:1], t["recv_c"][:, 0:1], t["cmaskv"][:, 0:1],
                ALU.mult), waits=[("q_gpsimd", rd), ("bC", self.c_consts)],
                inc="vector")
            v2 = self.I("vector", lambda e, hb=hb, fb=fb, ib=ib:
                e.tensor_tensor_scan(
                    hb[:], fb[:], ib[:], t["init_c"][:, 0:1],
                    ALU.mult, ALU.add), inc="vector")
            self.g[("scan2", m)] = v2
            self.g[("h_st", pb)] = self.dma(
                "sync", "bAh%d" % pb, t["dsc_ah"][m], hb[:],
                waits=[("vector", v2)])
            self._quant_w(t["in_w1"], t["dsc_wq1n"], t["sw1"],
                          4 * m, 4 * m + 4, D)
        h2_block(DC - 1)
        self.g["attn_mm_done"] = self.counts["tensor"]
        self.ph_s1b()
        v = self.I("vector", lambda e: e.tensor_scalar(
            t["sumh2"][0:1, :], t["sumh2"][0:1, :], 1.0 / D, None, ALU.mult),
            waits=[("vector", self.g["h2_add"])], inc="vector")
        sa = self.I("scalar", lambda e: e.activation(
            t["sumh2"][0:1, :], t["sumh2"][0:1, :], AF.Sqrt,
            bias=t["consv"][0:1, 1:2]), waits=[("vector", v)], inc="scalar")
        self.v_rstd2 = self.I("vector", lambda e: e.reciprocal(
            t["sumh2"][0:1, :], t["sumh2"][0:1, :]), waits=[("scalar", sa)],
            inc="vector")

        for m in range(DC):
            for blk in range(NTB_OWN):
                b = (m * NTB_OWN + blk) % 2
                g = self.g.get(("tl", b), {})
                ldh = self.dma("sync", "lh%d" % b, t["hrt"][b][:, 0:TB],
                               t["dsc_ah"][m][:, blk * TB:(blk + 1) * TB],
                               waits=[("vector", g.get("t1", self.v_castw)),
                                      ("bAh0", self.counts["bAh0"]),
                                      ("bAh1", self.counts["bAh1"])])
                ldg = self.dma("sync", "lg%d" % b, t["gs16"][b][:, 0:TB],
                               t["dsc_gs"][m][:, blk * TB:(blk + 1) * TB],
                               waits=[("vector", g.get("o", 0)),
                                      ("sg0", self.g.get(("gs_st", 0), 0)),
                                      ("sg1", self.g.get(("gs_st", 1), 0)),
                                      ("scalar", self.g.get("sg_last", 0))])
                pbk = blk % 2
                mmb = self.I("tensor", lambda e, blk=blk, pbk=pbk: e.matmul(
                    t["ps_bc"][pbk][:, 0:TB], t["ones1"][0:1, :],
                    t["sumh2"][0:1, blk * TB:(blk + 1) * TB],
                    start=True, stop=True),
                    waits=[("vector", self.v_rstd2),
                           ("vector", g.get("t1", 0))], inc="tensor")
                v1 = self.I("vector", lambda e, b=b, pbk=pbk: e.tensor_tensor(
                    t["t1t"][b][:, 0:TB], t["hrt"][b][:, 0:TB],
                    t["ps_bc"][pbk][:, 0:TB], ALU.mult),
                    waits=[("tensor", mmb), ("lh%d" % b, ldh),
                           ("scalar", g.get("t2", 0))], inc="vector")
                s2 = self.I("scalar", lambda e, b=b, m=m: e.activation(
                    t["t2t"][b][:, 0:TB], t["t1t"][b][:, 0:TB], AF.Identity,
                    scale=t["negnw"][:, m:m + 1]),
                    waits=[("vector", v1), ("vector", g.get("o", 0))],
                    inc="scalar")
                vo = self.I("vector", lambda e, b=b: e.tensor_tensor(
                    t["o16"][b][:, 0:TB], t["t2t"][b][:, 0:TB],
                    t["gs16"][b][:, 0:TB], ALU.mult),
                    waits=[("scalar", s2), ("lg%d" % b, ldg),
                           ("so%d" % b, g.get("ost", 0)),
                           ("ws0", self.c_cast[0]),
                           ("ws1", self.c_cast[1])], inc="vector")
                ost = self.dma("sync", "so%d" % b,
                               t["dsc_o"][m][:, blk * TB:(blk + 1) * TB],
                               t["o16"][b][:, 0:TB], waits=[("vector", vo)])
                self.g[("tl", b)] = {"t1": v1, "t2": s2, "o": vo, "ost": ost}
        self.c_o_sp = (self.counts["so0"], self.counts["so1"])
        self.v_tail = self.counts["vector"]
        self.g["tail_mm"] = self.counts["tensor"]

    def ph_ffn1(self):
        t = self.t
        for k in range(DC):
            for hh in range(2):
                self.dma("scalar", "bW1r",
                         t["wq1r"][k][:, hh * 2048:(hh + 1) * 2048],
                         t["dsc_wq1n"][hh * 2048:(hh + 1) * 2048,
                                       k * P:(k + 1) * P],
                         waits=[("ws0", self.counts["ws0"]),
                                ("ws1", self.counts["ws1"]),
                                ("tensor", self.g.get("attn_mm_done", 0)),
                                ("vector", self.g[("scan2", DC - 1)]),
                                ("scalar", self.g.get("h2_act", 0)),
                                ("bAh0", self.counts["bAh0"]),
                                ("bAh1", self.counts["bAh1"])],
                         transpose=True)
        c_wq1r = self.counts["bW1r"]
        own_col = 0
        for_col = 0
        vw = max(self.v_aq, self.g.get(("wq_g2", "scan"), 0),
                 self.g.get(("wq_mx", "scan"), 0))
        sa_x = [("wl0", self.counts["wl0"]), ("wl1", self.counts["wl1"])]
        for bx in range(2):
            gx = dict(self.g.get(("xt", bx), {}))
            gq = self.g.get(("wqs", "scan", bx), {})
            gx["act"] = max(gx.get("act", 0), gq.get("s", 0),
                            self.g.get(("wq_act", "scan"), 0))
            gx["dve"] = max(gx.get("dve", 0), gq.get("v", 0))
            self.g[("xt", bx)] = gx
        self.g[("wq_act", "ffn")] = self.counts["scalar"]
        self.g[("wq_mx", "ffn")] = self.counts["vector"]
        self.g[("wq_ld0", "ffn")] = [("lh0", self.counts["lh0"]),
                                     ("lh1", self.counts["lh1"])]
        self.g[("wq_g2x", "ffn")] = [("so0", self.counts["so0"]),
                                     ("so1", self.counts["so1"])]
        pm = {0: self._xq_tiles(0, 4, t["xqf_pb"][0], t["aq_t"], t["bq_t"],
                                relu0=True, vwait=vw, sa_extra=sa_x,
                                ld_extra=sa_x)}
        for tb in range(NTB_ALL):
            if tb + 1 < NTB_ALL:
                pm[tb + 1] = self._xq_tiles(
                    (tb + 1) * 4, 4, t["xqf_pb"][(tb + 1) % 2],
                    t["aq_t"], t["bq_t"], relu0=True, vwait=vw,
                    sa_extra=sa_x, ld_extra=sa_x)
            if tb % 2 == 1:
                self._quant_w(t["in_w2"], t["dsc_wq2n"], t["sw2"],
                              tb // 2, tb // 2 + 1, F, cfg="ffn")
            own = tb < NTB_OWN
            xq = t["xqf_pb"][tb % 2]
            for fk in range(FC):
                kq = [(t["wq1r"][k][:, fk * P:(fk + 1) * P],
                       xq[k]) for k in range(DC)]
                bank, mm = self.mmgroup(kq, waits=[
                    ("vector", pm[tb][-1]), ("bW1r", c_wq1r)])
                if own:
                    sb = (tb * FC + fk) % 2
                    sa = self.I("scalar", lambda e, fk=fk, bank=bank, sb=sb:
                        e.activation(t["hsp"][sb][:, 0:TB],
                                     t["ps_mm"][bank][:, 0:TB], AF.Relu,
                                     bias=t["b1q"][:, fk:fk + 1],
                                     scale=t["s1sw"][:, fk:fk + 1]),
                        waits=[("tensor", mm), ("vector", self.v_b1q),
                               ("st%d" % sb, self.g.get(("hsp_st", sb), 0)),
                               ("vector", self.g.get(("hsp_mx", sb), 0)),
                               ("ws0", self.counts["ws0"]),
                               ("ws1", self.counts["ws1"])],
                        inc="scalar")
                    self.bank_done(bank, "scalar", sa)
                    st = self.dma("sync", "st%d" % sb,
                                  t["dsc_h"][fk][:, tb * TB:(tb + 1) * TB],
                                  t["hsp"][sb][:, 0:TB],
                                  waits=[("scalar", sa)])
                    self.g[("hsp_st", sb)] = st
                    vm = self.I("vector", lambda e, sb=sb, c=own_col:
                        e.pool_max(t["hmaxs"][:, c:c + 1],
                                   t["hsp"][sb][:, 0:TB]),
                        waits=[("scalar", sa)], inc="vector")
                    self.g[("hsp_mx", sb)] = vm
                    own_col += 1
                else:
                    vm = self.I("vector", lambda e, bank=bank, c=for_col:
                        e.pool_max(t["imaxs"][:, c:c + 1],
                                   t["ps_mm"][bank][:, 0:TB]),
                        waits=[("tensor", mm)], inc="vector")
                    self.bank_done(bank, "vector", vm)
                    for_col += 1
        self.g["mm1_half"] = self.counts["tensor"]
        self.sc_ffn_end = self.counts["scalar"]
        self.n_own = own_col
        self.n_for = for_col
        self.v_ffn1 = self.counts["vector"]

    def ph_s2(self):
        t = self.t
        imaxs_h = t["imaxs_h"]
        for fk in range(FC):
            self.I("vector", lambda e, fk=fk: e.pool_max(
                t["imaxf"][:, fk:fk + 1],
                bass.AP(imaxs_h, fk, [[imaxs_h.shape[1], P], [FC, 12]])),
                waits=[("vector", self.v_ffn1)], inc="vector")
        self.I("vector", lambda e: e.tensor_tensor(
            t["imaxf"][:], t["imaxf"][:], t["s1sw"][:], ALU.mult), inc="vector")
        self.I("vector", lambda e: e.tensor_tensor(
            t["imaxf"][:], t["imaxf"][:], t["b1q"][:], ALU.add), inc="vector")
        self.I("vector", lambda e: e.tensor_scalar(
            t["imaxf"][:], t["imaxf"][:], 0.0, None, ALU.max), inc="vector")
        self.I("vector", lambda e: e.pool_max(
            t["red1"][:, 1:2], t["imaxf"][:]), inc="vector")
        self.I("vector", lambda e: e.pool_max(
            t["red1"][:, 2:3], t["hmaxs"][:, 0:self.n_own]),
            waits=[("vector", self.v_ffn1)], inc="vector")
        v3 = self.I("vector", lambda e: e.pool_max(
            t["red1"][:, 3:4], t["red1"][:, 1:3]), inc="vector")
        g1 = self.I("gpsimd", lambda e: e.reduce_max(
            t["scal"][0:1, 4:5], t["red1"][:, 3:4], AX.C),
            waits=[("vector", v3)], inc="gpsimd")
        self.W("gpsimd", [("gpsimd", g1)])
        self.send(9, t["scal"][0:1, 4:5], 1)
        rd = self.recv(9, t["exch2"][0:2, 0:1], t["shm_pay"][9])
        g2 = self.I("gpsimd", lambda e: e.reduce_max(
            t["scal"][0:1, 5:6], t["exch2"][0:2, 0:1], AX.C),
            waits=[("q_gpsimd", rd)], inc="gpsimd")
        self.I("vector", lambda e: e.tensor_scalar(
            t["scal"][0:1, 6:7], t["scal"][0:1, 5:6], EPS_Q, 1.0 / 255.0,
            ALU.max, ALU.mult), waits=[("gpsimd", g2)], inc="vector")
        v4 = self.I("vector", lambda e: e.reciprocal(
            t["scal"][0:1, 7:8], t["scal"][0:1, 6:7]), inc="vector")
        mm = self.I("tensor", lambda e: e.matmul(
            t["ps_bc"][0][:, 2:4], t["ones1"][0:1, :], t["scal"][0:1, 6:8],
            start=True, stop=True), waits=[("vector", v4)], inc="tensor")
        ve = self.I("vector", lambda e: e.tensor_copy(
            t["s1v"][:, 2:4], t["ps_bc"][0][:, 2:4]),
            waits=[("tensor", mm)], inc="vector")
        self.v_s2v = ve

        mmt = self.I("tensor", lambda e: e.transpose(
            t["ps_bc"][0][0:DC, 0:P], t["sw2"][:], t["eye32"][:]),
            waits=[("vector", ve)], inc="tensor")
        vev = self.I("vector", lambda e: e.tensor_copy(
            t["sw2r8"][0:DC, :], t["ps_bc"][0][0:DC, 0:P]),
            waits=[("tensor", mmt)], inc="vector")
        st = self.dma("sync", "bRow", t["dsc_row"], t["sw2r8"][0:DC, :],
                      waits=[("vector", vev)])
        ld = self.dma("sync", "bRow", t["sw2row"][0:1, :],
                      t["dsc_row"].rearrange("a b -> (a b)")[None, :],
                      waits=[("bRow", st),
                             ("tensor", self.g.get("tail_mm", 0)),
                             ("vector", self.v_tail)])
        v5 = self.I("vector", lambda e: e.tensor_scalar(
            t["sw2row"][0:1, :], t["sw2row"][0:1, :], t["scal"][0:1, 6:7],
            None, ALU.mult), waits=[("bRow", ld), ("vector", ve)],
            inc="vector")
        self.I("vector", lambda e: e.reciprocal(
            t["b2qrow"][0:1, :], t["sw2row"][0:1, :]), inc="vector")
        self.I("vector", lambda e: e.tensor_tensor(
            t["b2qrow"][0:1, :], t["b2row"][0:1, :], t["b2qrow"][0:1, :],
            ALU.mult), waits=[("bC", self.c_consts)], inc="vector")
        self.I("vector", lambda e: e.tensor_scalar(
            t["b2qrow"][0:1, :], t["b2qrow"][0:1, :], MAGIC, MAGIC, ALU.add,
            ALU.subtract), inc="vector")
        v6 = self.I("vector", lambda e: e.tensor_tensor(
            t["b2qrow"][0:1, :], t["b2qrow"][0:1, :], t["sw2row"][0:1, :],
            ALU.mult), inc="vector")
        prev = v6
        first = True
        for (row, dstt) in ((t["sw2row"], t["s2swb"]), (t["b2qrow"], t["b2qb"])):
            for hh in range(2):
                mmb = self.I("tensor", lambda e, row=row, hh=hh: e.matmul(
                    t["ps_bc"][1][:, 0:TB], t["ones1"][0:1, :],
                    row[0:1, hh * TB:(hh + 1) * TB], start=True, stop=True),
                    waits=[("vector", prev), ("vector", self.v_tail)],
                    inc="tensor")
                w = [("tensor", mmb)]
                if first:
                    w += [("so0", self.c_o_sp[0]), ("so1", self.c_o_sp[1]),
                          ("bAh0", self.counts["bAh0"]),
                          ("bAh1", self.counts["bAh1"]),
                          ("vector", self.v_tail),
                          ("scalar", self.g.get("h2_act", 0))]
                    first = False
                prev = self.I("vector", lambda e, dstt=dstt, hh=hh: e.tensor_copy(
                    dstt[:, hh * TB:(hh + 1) * TB], t["ps_bc"][1][:, 0:TB]),
                    waits=w, inc="vector")
        self.v_bc2 = prev

    def ph_final(self):
        t = self.t
        for fk in range(FC):
            self.dma("scalar", "bW2r", t["wq2r"][fk][:],
                     t["dsc_wq2n"][0:D, fk * P:(fk + 1) * P],
                     waits=[("ws0", self.counts["ws0"]),
                            ("ws1", self.counts["ws1"]),
                            ("tensor", self.g.get("mm1_half", 0))],
                     transpose=True)
        c_wq2r = self.counts["bW2r"]
        for k in range(DC):
            self.dma("scalar", "bWo", t["wto"][k][:],
                     t["dsc_wo"][0:D, k * P:(k + 1) * P],
                     waits=[("ws0", self.c_cast[0]),
                            ("ws1", self.c_cast[1]),
                            ("scalar", self.sc_ffn_end),
                            ("bAh0", self.counts["bAh0"]),
                            ("bAh1", self.counts["bAh1"])], transpose=True)
        c_wto = self.counts["bWo"]

        pool = t["hqpool"]
        npool = len(pool)
        dtot = [(ch, self.counts[ch]) for ch in
                ("st0", "st1", "so0", "so1", "lg0", "lg1", "ws0", "ws1")]
        E = []
        base_war = self.g["mm1_half"]
        for tb in range(NTB_OWN):
            late = {}
            ts2m = {}
            for fk in range(FC):
                slot = fk % 4
                b = fk % 2
                idx = tb * FC + fk
                pi, idx2 = idx % npool, idx - npool
                war = base_war if idx2 < 0 else E[idx2 // FC]
                late[fk] = (idx2 >= 0 and idx2 // FC == tb - 1)
                gs = self.g.get(("hqs", slot), None)
                if gs is None:
                    ldw = [("scalar",
                            max(self.g.get(("xt", 0), {}).get("act", 0),
                                self.g.get(("xt", 1), {}).get("act", 0))),
                           ("vector", self.v_ffn1)]
                else:
                    ldw = [("scalar", gs)]
                ld = self.dma("sync", "xh%d" % slot, t["hqld"][slot],
                              t["dsc_h"][fk][:, tb * TB:(tb + 1) * TB],
                              waits=ldw + [("st0", self.counts["st0"]),
                                           ("st1", self.counts["st1"])])
                sa = self.I("scalar", lambda e, slot=slot, b=b: e.activation(
                    t["hqy"][b], t["hqld"][slot], AF.Identity,
                    bias=t["consv"][:, 0:1], scale=t["s1v"][:, 3:4]),
                    waits=[("xh%d" % slot, ld), ("vector", self.v_s2v),
                           ("vector", self.g.get(("hqyw", b), 0)),
                           ("st0", self.counts["st0"]),
                           ("st1", self.counts["st1"])], inc="scalar")
                self.g[("hqs", slot)] = sa
                v2 = self.I("vector", lambda e, pi=pi, b=b: e.tensor_scalar(
                    pool[pi][:, 0:TB], t["hqy"][b], MAGIC, None, ALU.subtract),
                    waits=[("scalar", sa), ("tensor", war)] + dtot,
                    inc="vector")
                self.g[("hqyw", b)] = v2
                ts2m[fk] = v2
            fk_order = sorted(range(FC), key=lambda fk: late[fk])
            for k in range(DC):
                self.dma("sync", "bOf", t["of"][k][:, 0:TB],
                         t["dsc_o"][k][:, tb * TB:(tb + 1) * TB],
                         waits=[("so0", self.c_o_sp[0]), ("so1", self.c_o_sp[1]),
                                ("tensor", self.g.get("mm2", 0)),
                                ("vector", self.v_tail)])
            c_of = self.counts["bOf"]
            for tc in range(4):
                for dh in range(2):
                    i = tb * 4 + tc
                    self.dma("sync", "bXre", t["xre"][tc * 2 + dh],
                             t["in_xst"][i * P:(i + 1) * P,
                                         dh * TB:(dh + 1) * TB],
                             waits=[("vector", self.g.get("fin", self.v_ffn1)),
                                    ("vector", self.v_tail),
                                    ("scalar", self.sc_ffn_end),
                                    ("wl0", self.counts["wl0"]),
                                    ("wl1", self.counts["wl1"])])
            c_xre = self.counts["bXre"]
            for tc in range(4):
                tcg = tb * 4 + tc
                for dh in range(2):
                    kb = [(pool[(tb * FC + fk) % npool][:, tc * P:(tc + 1) * P],
                           t["wq2r"][fk][:, dh * TB:(dh + 1) * TB])
                          for fk in fk_order]
                    bankb, mmb = self.mmgroup(kb, waits=[
                        ("bW2r", c_wq2r)],
                        mm_waits=[[("vector", ts2m[fk])] for fk in fk_order])
                    ka = [(t["of"][k][:, tc * P:(tc + 1) * P],
                           t["wto"][k][:, dh * TB:(dh + 1) * TB])
                          for k in range(DC)]
                    banka, mma = self.mmgroup(ka, waits=[
                        ("bOf", c_of), ("bWo", c_wto)])
                    fb = (tc * 2 + dh) % 2
                    gf = self.g.get(("fin", fb), {})
                    v1 = self.I("vector", lambda e, fb=fb, dh=dh, bankb=bankb:
                        e.tensor_tensor(t["finu"][fb][:, 0:TB],
                                        t["ps_mm"][bankb][:, 0:TB],
                                        t["s2swb"][:, dh * TB:(dh + 1) * TB],
                                        ALU.mult),
                        waits=[("tensor", mmb), ("ou%d" % fb, gf.get("st", 0))],
                        inc="vector")
                    self.I("vector", lambda e, fb=fb, dh=dh: e.tensor_tensor(
                        t["finu"][fb][:, 0:TB], t["finu"][fb][:, 0:TB],
                        t["b2qb"][:, dh * TB:(dh + 1) * TB], ALU.add),
                        inc="vector")
                    v3 = self.I("vector", lambda e, fb=fb, banka=banka:
                        e.tensor_tensor(t["finu"][fb][:, 0:TB],
                                        t["finu"][fb][:, 0:TB],
                                        t["ps_mm"][banka][:, 0:TB], ALU.add),
                        waits=[("tensor", mma)], inc="vector")
                    self.bank_done(banka, "vector", v3)
                    self.bank_done(bankb, "vector", v1)
                    v4 = self.I("vector", lambda e, fb=fb, tc=tc, dh=dh:
                        e.tensor_tensor(t["outst"][fb][:, 0:TB],
                                        t["finu"][fb][:, 0:TB],
                                        t["xre"][tc * 2 + dh][:, 0:TB],
                                        ALU.add),
                        waits=[("bXre", c_xre)], inc="vector")
                    self.g["fin"] = v4
                    st = self.dma("sync", "ou%d" % fb,
                                  t["out"][tcg * P:(tcg + 1) * P,
                                           dh * TB:(dh + 1) * TB],
                                  t["outst"][fb][:, 0:TB],
                                  waits=[("vector", v4)])
                    self.g[("fin", fb)] = {"st": st}
            self.g["mm2"] = self.counts["tensor"]
            E.append(self.counts["tensor"])


def build_program(taps=()):
    nc = bass.Bass("TRN2", target_bir_lowering=False)
    t = {}

    def dram(name, shape, dtype=F32, **kw):
        t[name] = nc.dram_tensor(name, shape, dtype, **kw).ap()

    dram("in_xst", [T_OWN + T_FOR, D], kind="ExternalInput")
    for w in ("wi", "wf", "wg", "wo"):
        dram("in_" + w, [D, D], kind="ExternalInput")
    dram("in_w1", [F, D], kind="ExternalInput")
    dram("in_w2", [D, F], kind="ExternalInput")
    dram("in_b1t", [P, FC], kind="ExternalInput")
    dram("in_b2row", [1, D], kind="ExternalInput")
    dram("in_nwt", [P, DC], kind="ExternalInput")
    dram("in_eye16", [P, P], BF16, kind="ExternalInput")
    dram("in_eye32", [P, P], kind="ExternalInput")
    dram("in_ones1", [1, P], kind="ExternalInput")
    dram("in_onesb", [P, 1], BF16, kind="ExternalInput")
    dram("in_cmaskv", [P, 1], kind="ExternalInput")
    dram("in_consv", [P, 2], kind="ExternalInput")
    dram("in_nonces_i", [1, 2 * NEX], I32, kind="ExternalInput")
    dram("in_parity_i", [1, 1], I32, kind="ExternalInput")
    dram("out", [T_OWN, D], kind="ExternalOutput")
    for w in ("wi", "wf", "wg", "wo"):
        dram("dsc_" + w, [D, D], BF16)
    dram("dsc_wq1n", [F, D], BF16)
    dram("dsc_wq2n", [D, F], BF16)
    dram("dsc_row", [DC, P])
    t["dsc_h"] = [nc.dram_tensor("dsc_h%d" % fk, [P, T_OWN], F32).ap()
                  for fk in range(FC)]
    t["dsc_ah"] = [nc.dram_tensor("dsc_ah%d" % m, [P, T_OWN], F32).ap()
                   for m in range(DC)]
    t["dsc_gs"] = [nc.dram_tensor("dsc_gs%d" % m, [P, T_OWN], BF16).ap()
                   for m in range(DC)]
    t["dsc_o"] = [nc.dram_tensor("dsc_o%d" % m, [P, T_OWN], BF16).ap()
                  for m in range(DC)]
    t["shm_pay"] = []
    for ex in range(NEX):
        rows = 1 if ex in (0, 9) else P
        t["shm_pay"].append(nc.dram_tensor("shm_pay%d" % ex, [2 * rows, 1],
                                           F32, addr_space="Shared").ap())
    t["shm_flag"] = nc.dram_tensor("shm_flag", [1, 2 * NEX], I32,
                                   addr_space="Shared").ap()

    ctx = ExitStack()

    def sb(name, shape, dtype=F32):
        t[name] = ctx.enter_context(nc.sbuf_tensor(name, shape, dtype))
        return t[name]

    def sblist(name, n, shape, dtype=F32):
        t[name] = [ctx.enter_context(nc.sbuf_tensor("%s%d" % (name, i), shape,
                                                    dtype)) for i in range(n)]

    sb("eye16", [P, P], BF16)
    sb("eye32", [P, P])
    sb("ones1", [1, P]); sb("onesb", [P, 1], BF16)
    sb("b1t", [P, FC]); sb("b2row", [1, D]); sb("nwt", [P, DC])
    sb("negnw", [P, DC]); sb("cmaskv", [P, 1]); sb("consv", [P, 2])
    sb("nonces_i", [1, 2 * NEX], I32); sb("parity_i", [1, 1], I32)
    sb("pollf", [1, 2], I32)
    sb("scal", [1, 16]); sb("exch2", [2, 1]); sb("recv_c", [P, 1])
    sb("init_c", [P, 1]); sb("red1", [P, 4])
    for n in ("sumc", "sqc", "xmaxc", "m_t", "var_t", "sd_t", "rstd_t",
              "xbias_t", "cand_t", "aq_t", "bq_t"):
        sb(n, [P, NT_ST])
    sb("sw1", [P, FC]); sb("sw2", [P, DC]); sb("s1sw", [P, FC])
    sb("b1q", [P, FC]); sb("s1v", [P, 4]); sb("wred", [P, 10])
    sb("sumh2", [1, T_OWN])
    sb("sw2r8", [DC, P])
    t["sw2row"] = t["sumh2"][0:1, 0:D]
    t["b2qrow"] = t["sumh2"][0:1, D:2 * D]
    sb("imaxf", [P, FC])
    sb("hmaxs", [P, 128])
    t["imaxs_h"] = sb("imaxs", [P, 12 * FCH * 2])
    sblist("xtile", 2, [P, D])
    sblist("y16", 2, [P, D], BF16)
    sb("scr_a", [P, D]); sb("scr_v", [P, D])
    t["y32"] = [t["scr_a"][:, 0:D], t["scr_v"][:, 0:D]]
    sblist("xnb", DC, [P, T_OWN], BF16)
    finbig = sb("finbig", [P, 4096])
    t["fbuf"] = finbig[:, 0:2048]
    t["ibuf"] = finbig[:, 2048:4096]
    fin_addr = nc.lookup_mloc(finbig).addr
    t["of"] = [nc.alloc_sbuf_tensor_at("of%d" % k, [P, TB], BF16,
                                       offset=fin_addr + k * TB * 2)
               for k in range(DC)]
    t["finu"] = [nc.alloc_sbuf_tensor_at("finu%d" % i, [P, TB], F32,
                                         offset=fin_addr + 8192 + i * TB * 4)
                 for i in range(2)]
    t["outst"] = [nc.alloc_sbuf_tensor_at("outst%d" % i, [P, TB], F32,
                                          offset=fin_addr + 12288 + i * TB * 4)
                  for i in range(2)]
    t["fbuf"] = t["fbuf"]
    sb("sib", [P, TB])
    hrot = sb("hrot", [P, T_OWN])
    t["s2swb"] = hrot[:, 0:D]
    t["b2qb"] = hrot[:, D:2 * D]
    t["hrot"] = hrot
    sb("h2t", [P, TB], BF16)
    sblist("gs16", 2, [P, TB], BF16)
    sblist("hrt", 2, [P, TB]); sblist("t1t", 2, [P, TB]); sblist("t2t", 2, [P, TB])
    sblist("o16", 2, [P, TB], BF16)
    sblist("xqf", DC, [P, TB], BF16)
    sblist("hsp", 2, [P, TB])
    t["hqld"] = [t["scr_a"][:, 0:TB], t["scr_a"][:, TB:2 * TB],
                 t["scr_v"][:, 0:TB], t["scr_v"][:, TB:2 * TB]]
    t["hqy"] = [t["t2t"][0][:, 0:TB], t["t2t"][1][:, 0:TB]]
    pool = [t["xnb"][fk % DC][:, (fk // DC) * TB:(fk // DC + 1) * TB]
            for fk in range(FC)]
    pool += [t["xqf"][k][:, 0:TB] for k in range(DC)]
    for nm in ("hsp0", "hsp1", "sib"):
        src = t["hsp"][0] if nm == "hsp0" else (
            t["hsp"][1] if nm == "hsp1" else t["sib"])
        ad = nc.lookup_mloc(src).addr
        for hh in range(2):
            pool.append(nc.alloc_sbuf_tensor_at(
                "hqp_%s_%d" % (nm, hh), [P, TB], BF16, offset=ad + hh * TB * 2))
    t["wldx"] = [t["xtile"][0], t["xtile"][1]]
    t["wld1"] = t["scr_v"]
    t["wldx_f"] = [t["hrt"][0], t["hrt"][1]]
    t["wld1_f"] = t["t1t"][0]
    t["wq16_f"] = [t["o16"][0], t["o16"][1]]
    t["wq16"] = [nc.alloc_sbuf_tensor_at(
        "wq16v%d" % b, [P, D], BF16,
        offset=nc.lookup_mloc(t["hsp"][b]).addr) for b in range(2)]
    pool += [t["gs16"][0][:, 0:TB], t["gs16"][1][:, 0:TB],
             t["o16"][0][:, 0:TB], t["o16"][1][:, 0:TB], t["h2t"][:, 0:TB]]
    t["hqpool"] = pool
    t["xre"] = [t["xtile"][0][:, 0:TB], t["xtile"][0][:, TB:2 * TB],
                t["xtile"][1][:, 0:TB], t["xtile"][1][:, TB:2 * TB],
                t["hrt"][0][:, 0:TB], t["hrt"][1][:, 0:TB],
                t["t1t"][0][:, 0:TB], t["t1t"][1][:, 0:TB]]
    wbig = sb("wbig", [P, 20480])
    wb = nc.lookup_mloc(wbig).addr

    def at(name, shape, dtype, off):
        return nc.alloc_sbuf_tensor_at(name, shape, dtype, offset=wb + off)

    t["wti"] = [at("wti%d" % k, [P, D], BF16, k * 2048) for k in range(DC)]
    t["wtf"] = [at("wtf%d" % k, [P, D], BF16, 16384 + k * 2048) for k in range(DC)]
    t["wtg"] = [at("wtg%d" % k, [P, D], BF16, 32768 + k * 2048) for k in range(DC)]
    t["wq1r"] = [at("wq1r%d" % k, [P, F], BF16, k * 8192) for k in range(DC)]
    t["fbuf2"] = at("fbuf2", [P, T_OWN], F32, 49152)
    t["ibuf2"] = at("ibuf2", [P, T_OWN], F32, 57344)
    t["hrot2"] = at("hrot2", [P, T_OWN], F32, 65536)
    t["wq2r"] = [at("wq2r%d" % fk, [P, D], BF16, fk * 2048) for fk in range(FC)]
    t["wto"] = [at("wto%d" % k, [P, D], BF16, 65536 + k * 2048) for k in range(DC)]
    t["xqf_pb"] = [[t["xnb"][k][:, b * TB:(b + 1) * TB] for k in range(DC)]
                   for b in range(2)]

    t["ps_mm"] = [ctx.enter_context(nc.psum_tensor("psmm%d" % i, [P, TB], F32))
                  for i in range(4)]
    t["ps_bc"] = [ctx.enter_context(nc.psum_tensor("psbc%d" % i, [P, TB], F32))
                  for i in range(2)]
    t["ps_tr"] = [ctx.enter_context(nc.psum_tensor("pstr%d" % i, [P, P], BF16))
                  for i in range(2)]
    t["ps_row"] = t["ps_bc"][1]

    sems = {}
    for name in ("q_gpsimd", "q_poll", "bC", "bCast", "bWt", "bW1r", "bW2r",
                 "bWo", "bAh0", "bAh1", "bRow", "bOf", "bXre",
                 "wl0", "wl1", "ws0", "ws1", "xt0", "xt1",
                 "xh0", "xh1", "xh2", "xh3",
                 "st0", "st1", "sg0", "sg1", "so0", "so1", "lh0", "lh1",
                 "lg0", "lg1", "ou0", "ou1",
                 "tensor", "vector", "scalar", "gpsimd"):
        sems[name] = ctx.enter_context(nc.semaphore(name))

    prog = Prog(nc, t, sems, taps)
    with nc.Block() as block:
        @block.sync
        def _(eng):
            prog.walk("sync", eng)

        @block.scalar
        def _(eng):
            prog.walk("scalar", eng)

        @block.vector
        def _(eng):
            prog.walk("vector", eng)

        @block.tensor
        def _(eng):
            prog.walk("tensor", eng)

        @block.gpsimd
        def _(eng):
            prog.walk("gpsimd", eng)
    ctx.close()
    return nc, t


def host_inputs(x, Wi, Wf, Wg, Wo, nw, w1, b1, w2, b2):
    xf = np.ascontiguousarray(np.asarray(x, np.float32).reshape(NCORE * T_OWN, D))
    eye = np.eye(P).astype(ml_dtypes.bfloat16)
    eye32 = np.eye(P).astype(np.float32)
    ones1 = np.ones((1, P), np.float32)
    onesb = np.ones((P, 1)).astype(ml_dtypes.bfloat16)
    b1t = np.ascontiguousarray(np.asarray(b1, np.float32).reshape(FC, P).T)
    b2row = np.ascontiguousarray(np.asarray(b2, np.float32).reshape(1, D))
    nwt = np.ascontiguousarray(np.asarray(nw, np.float32).reshape(DC, P).T)
    rng = np.random.default_rng()
    consv = np.tile(np.array([[MAGIC, EPS_LN]], np.float32), (P, 1))
    nonces = rng.integers(1, 2 ** 30, size=(1, 2 * NEX)).astype(np.int32)
    Wi, Wf, Wg, Wo = [np.ascontiguousarray(np.asarray(a, np.float32))
                      for a in (Wi, Wf, Wg, Wo)]
    w1 = np.ascontiguousarray(np.asarray(w1, np.float32))
    w2 = np.ascontiguousarray(np.asarray(w2, np.float32))
    in_maps = []
    for c in range(NCORE):
        own = np.arange(c * T_OWN, (c + 1) * T_OWN)
        pair = c // 2
        forall = np.concatenate([np.arange(0, pair * 2 * T_OWN),
                                 np.arange((pair + 1) * 2 * T_OWN,
                                           NCORE * T_OWN)])
        fsl = forall[(c % 2) * T_FOR:(c % 2 + 1) * T_FOR]
        xst = np.ascontiguousarray(np.concatenate([xf[own], xf[fsl]], axis=0))
        in_maps.append(dict(
            in_xst=xst, in_wi=Wi, in_wf=Wf, in_wg=Wg, in_wo=Wo,
            in_w1=w1, in_w2=w2, in_b1t=b1t, in_b2row=b2row, in_nwt=nwt,
            in_eye16=eye, in_eye32=eye32, in_ones1=ones1, in_onesb=onesb,
            in_cmaskv=np.full((P, 1), float(c % 2), np.float32),
            in_consv=consv,
            in_nonces_i=nonces,
            in_parity_i=np.array([[c % 2]], np.int32),
        ))
    return in_maps


def kernel(L0_attn_x, L0_attn_Wi, L0_attn_Wf, L0_attn_Wg, L0_attn_Wo,
           L0_attn_norm_w, L0_ffn_w1, L0_ffn_b1, L0_ffn_w2, L0_ffn_b2):
    in_maps = host_inputs(L0_attn_x, L0_attn_Wi, L0_attn_Wf, L0_attn_Wg,
                          L0_attn_Wo, L0_attn_norm_w, L0_ffn_w1, L0_ffn_b1,
                          L0_ffn_w2, L0_ffn_b2)
    nc, _ = build_program()
    res = run_bass_kernel_spmd(nc, in_maps, core_ids=list(range(NCORE)))
    out = np.concatenate([np.asarray(res.results[c]["out"])
                          for c in range(NCORE)], axis=0)
    return out.reshape(4, 4096, D)


# version 12
# speedup vs baseline: 1.4161x; 1.4161x over previous
import numpy as np
import ml_dtypes
from contextlib import ExitStack

import concourse.bass as bass
import concourse.mybir as mybir
from concourse.bass_utils import run_bass_kernel_spmd

F32 = mybir.dt.float32
BF16 = mybir.dt.bfloat16
I32 = mybir.dt.int32
ALU = mybir.AluOpType
AF = mybir.ActivationFunctionType
AX = mybir.AxisListType

P = 128
D = 1024
DC = D // P
F = 4096
FC = F // P
FCH = FC // 2
T_OWN = 2048
NT_OWN = T_OWN // P
T_FOR = 6144
NT_ST = (T_OWN + T_FOR) // P
TB = 512
NTB_ALL = (T_OWN + T_FOR) // TB
NTB_OWN = T_OWN // TB
MAGIC = 12582912.0
EPS_LN = 1e-5
EPS_Q = 1e-8
NCORE = 8
NEX = 10


class Prog:
    def __init__(self, nc, t, sems, taps=()):
        self.nc, self.t, self.sems = nc, t, sems
        self.taps = set(taps)

    def walk(self, engine, eng):
        self.engine, self.eng = engine, eng
        self.regn = 0
        self.poll_reg = None
        self.counts = {k: 0 for k in self.sems}
        self.waited = {k: 0 for k in self.sems}
        self.g = {}
        self.bank_guard = [[] for _ in range(4)]
        self.ps_ctr = 0
        self.tr_ctr = 0
        self.wq_pc = 0
        self.program()

    def I(self, engine, emit, waits=(), inc=None, inc_n=1):
        if engine in ("vector", "scalar", "gpsimd"):
            waits = list(waits) + [(engine, self.counts[engine])]
        if inc is not None:
            self.counts[inc] += inc_n
        if self.engine == engine:
            for ch, v in waits:
                if v > self.waited[ch]:
                    self.eng.wait_ge(self.sems[ch], v)
                    self.waited[ch] = v
            ins = emit(self.eng)
            if inc is not None:
                ins.then_inc(self.sems[inc], inc_n)
        return self.counts[inc] if inc is not None else None

    def W(self, engine, waits):
        if self.engine == engine:
            for ch, v in waits:
                if v > self.waited[ch]:
                    self.eng.wait_ge(self.sems[ch], v)
                    self.waited[ch] = v

    def dma(self, q, sem, out, in_, waits=(), **kw):
        return self.I(q, lambda e: e.dma_start(out=out, in_=in_, **kw),
                      waits=waits, inc=sem, inc_n=16)

    def mmgroup(self, kmms, waits=(), mm_waits=None):
        bank = self.ps_ctr % 4
        self.ps_ctr += 1
        w = list(waits) + self.bank_guard[bank]
        self.bank_guard[bank] = []
        mark = None
        n = len(kmms)
        for k, (lh, rh) in enumerate(kmms):
            w_k = list(w) if k == 0 else []
            if mm_waits is not None:
                w_k += list(mm_waits[k])
            mark = self.I("tensor", lambda e, lh=lh, rh=rh, k=k, n=n, bank=bank:
                          e.matmul(self.t["ps_mm"][bank][:, 0:TB], lh, rh,
                                   start=(k == 0), stop=(k == n - 1)),
                          waits=w_k,
                          inc="tensor" if k == n - 1 else None)
        return bank, mark

    def bank_done(self, bank, chan, mark):
        self.bank_guard[bank].append((chan, mark))

    def ph_warm(self):
        t = self.t
        self.I("gpsimd", lambda e: e.memset(t["init_c"][:, 0:1], 0.0),
               waits=[("bC", self.c_consts)], inc="gpsimd")
        self.I("gpsimd", lambda e: e.memset(t["pollf"][0:1, 0:2], 0.0),
               inc="gpsimd")
        self.W("gpsimd", [("gpsimd", self.counts["gpsimd"])])
        for ex in range(NEX):
            rows = 1 if ex in (0, 9) else P
            self.counts["q_gpsimd"] += 16
            if self.engine == "gpsimd":
                e = self.eng
                self.regn += 1
                with e.register("wp%d" % self.regn) as par:
                    e.reg_load(par, t["parity_i"][0:1, 0:1])
                    for s in range(2):
                        with e.If_eq(par, s):
                            e.dma_start(
                                out=t["shm_pay"][ex][s * rows:(s + 1) * rows, :],
                                in_=t["init_c"][0:rows, 0:1]
                                ).then_inc(self.sems["q_gpsimd"], 16)
        self.W("gpsimd", [("q_gpsimd", self.counts["q_gpsimd"])])
        self.counts["q_poll"] += 16
        if self.engine == "gpsimd":
            e = self.eng
            self.regn += 1
            with e.register("wr%d" % self.regn) as par:
                e.reg_load(par, t["parity_i"][0:1, 0:1])
                for s in range(2):
                    with e.If_eq(par, s):
                        e.dma_start(out=t["pollf"][0:1, 0:1],
                                    in_=t["shm_flag"][0:1, s:s + 1]
                                    ).then_inc(self.sems["q_poll"], 16)
            self.poll_reg = e.alloc_register("pollcnt")
            e.reg_mov(self.poll_reg, 16)
        self.W("gpsimd", [("q_poll", self.counts["q_poll"])])

    def send(self, ex, payload_ap, rows):
        t = self.t
        shd = t["shm_pay"][ex]
        self.counts["q_gpsimd"] += 16
        pay_done = self.counts["q_gpsimd"]
        if self.engine == "gpsimd":
            e = self.eng
            self.regn += 1
            with e.register("par%d" % self.regn) as par:
                e.reg_load(par, t["parity_i"][0:1, 0:1])
                for s in range(2):
                    with e.If_eq(par, s):
                        e.dma_start(out=shd[s * rows:(s + 1) * rows, :],
                                    in_=payload_ap
                                    ).then_inc(self.sems["q_gpsimd"], 16)
            e.wait_ge(self.sems["q_gpsimd"], pay_done)
            self.waited["q_gpsimd"] = pay_done
        self.counts["q_gpsimd"] += 16
        if self.engine == "gpsimd":
            e = self.eng
            self.regn += 1
            with e.register("parf%d" % self.regn) as par:
                e.reg_load(par, t["parity_i"][0:1, 0:1])
                for s in range(2):
                    with e.If_eq(par, s):
                        e.dma_start(
                            out=t["shm_flag"][0:1, 2 * ex + s:2 * ex + s + 1],
                            in_=t["nonces_i"][0:1, 2 * ex + s:2 * ex + s + 1]
                            ).then_inc(self.sems["q_gpsimd"], 16)

    def recv(self, ex, recv_sb, src_ap):
        t = self.t
        self.W("gpsimd", [("q_gpsimd", self.counts["q_gpsimd"])])
        if self.engine == "gpsimd":
            e = self.eng
            if getattr(self, "poll_reg", None) is None:
                self.poll_reg = e.alloc_register("pollcnt")
                e.reg_mov(self.poll_reg, 0)
            cnt = self.poll_reg
            self.regn += 1
            rn = self.regn
            with e.register("w%d" % rn) as w, \
                 e.register("x%d" % rn) as x, e.register("m%d" % rn) as miss, \
                 e.register("ne%d" % rn) as ne:

                def cond():
                    e.reg_add(cnt, cnt, 16)
                    e.dma_start(out=t["pollf"][0:1, 0:2],
                                in_=t["shm_flag"][0:1, 2 * ex:2 * ex + 2]
                                ).then_inc(self.sems["q_poll"], 16)
                    e.wait_ge(self.sems["q_poll"], cnt)
                    e.reg_mov(miss, 0)
                    for s in range(2):
                        e.reg_load(w, t["pollf"][0:1, s:s + 1])
                        e.reg_load(x, t["nonces_i"][0:1,
                                                    2 * ex + s:2 * ex + s + 1])
                        e.reg_alu(ne, w, x, ALU.not_equal)
                        e.reg_add(miss, miss, ne)
                    return miss
                with e.While(cond):
                    pass
        return self.dma("gpsimd", "q_gpsimd", recv_sb, src_ap)

    def program(self):
        self.ph_consts()
        self.ph_warm()
        self.ph_cast()
        self.ph_stats()
        self.ph_s1()
        self.ph_attn()
        self.ph_ffn1()
        self.ph_s2()
        self.ph_final()
        self.W("sync", [("ou0", self.counts["ou0"]),
                        ("ou1", self.counts["ou1"])])

    def ph_consts(self):
        t = self.t
        for name in ("eye16", "eye32", "ones1", "onesb", "b1t", "b2row", "nwt",
                     "cmaskv", "consv", "nonces_i", "parity_i"):
            self.dma("sync", "bC", t[name][:], t["in_" + name])
        self.c_consts = self.counts["bC"]
        self.v_negnw = self.I("vector", lambda e: e.tensor_scalar(
            t["negnw"][:], t["nwt"][:], -1.0, None, ALU.mult),
            waits=[("bC", self.c_consts)], inc="vector")

    def ph_cast(self):
        t = self.t
        for wi, w in enumerate(("wi", "wf", "wg", "wo")):
            for kk in range(2 * DC):
                k = wi * 2 * DC + kk
                b = k % 2
                r, c = kk // 2, kk % 2
                ld = self.dma("scalar", "wl%d" % b, t["hrt"][b][:, 0:TB],
                              t["in_" + w][r * P:(r + 1) * P,
                                           c * TB:(c + 1) * TB],
                              waits=[("vector", self.g.get(("cst", b), 0))])
                cp = self.I("vector", lambda e, b=b: e.tensor_copy(
                    t["o16"][b][:, 0:TB], t["hrt"][b][:, 0:TB]),
                    waits=[("wl%d" % b, ld),
                           ("ws%d" % b, self.g.get(("wqst", b), 0))],
                    inc="vector")
                self.g[("cst", b)] = cp
                st = self.dma("scalar", "ws%d" % b,
                              t["dsc_" + w][r * P:(r + 1) * P,
                                            c * TB:(c + 1) * TB],
                              t["o16"][b][:, 0:TB], waits=[("vector", cp)])
                self.g[("wqst", b)] = st
        self.c_cast = (self.counts["ws0"], self.counts["ws1"])
        self.v_castw = self.counts["vector"]

    def ph_wt(self):
        t = self.t
        for w, buf in (("wi", "wti"), ("wf", "wtf"), ("wg", "wtg")):
            for k in range(DC):
                self.dma("scalar", "bWt", t[buf][k][:],
                         t["dsc_" + w][0:D, k * P:(k + 1) * P],
                         waits=[("ws0", self.c_cast[0]),
                                ("ws1", self.c_cast[1])], transpose=True)
        self.c_wt = self.counts["bWt"]

    def _quant_w(self, src, dst, sw_tile, k0, k1, width):
        t = self.t
        hw = 1024
        halves = width // hw

        def wload(h, extra=()):
            pc = self.wq_pc
            self.wq_pc += 1
            b2 = pc % 2
            gsl = self.g.get(("wqs", b2), {})
            ld = self.dma("scalar", "wl%d" % b2, t["wldx"][b2][:, 0:hw],
                          src[k * P:(k + 1) * P, h * hw:(h + 1) * hw],
                          waits=[("scalar",
                                  gsl.get("s", self.g.get("wq_act", 0))),
                                 ("vector",
                                  gsl.get("v", self.g.get("wq_mx", 0)))] +
                          list(extra))
            return b2, ld

        for k in range(k0, k1):
            b = k % 2
            for h in range(halves):
                b2, ld = wload(h)
                vh = self.I("vector", lambda e, h=h, hw=hw, b2=b2:
                            e.reduce_max(t["wred"][:, h:h + 1],
                                         t["wldx"][b2][:, 0:hw], AX.X,
                                         apply_absolute_value=True),
                            waits=[("wl%d" % b2, ld)], inc="vector")
                g = dict(self.g.get(("wqs", b2), {}))
                g["v"] = vh
                self.g[("wqs", b2)] = g
            self.g["wq_mx"] = vh
            for h in range(1, halves):
                self.I("vector", lambda e, h=h: e.tensor_tensor(
                    t["wred"][:, 0:1], t["wred"][:, 0:1], t["wred"][:, h:h + 1],
                    ALU.max), inc="vector")
            self.I("vector", lambda e, k=k: e.tensor_scalar(
                sw_tile[:, k:k + 1], t["wred"][:, 0:1], EPS_Q, 1.0 / 127.0,
                ALU.max, ALU.mult), inc="vector")
            vr = self.I("vector", lambda e, k=k: e.reciprocal(
                t["wred"][:, 4:5], sw_tile[:, k:k + 1]), inc="vector")
            for h in range(halves):
                if halves > 1:
                    b2, ld = wload(h, extra=[("vector", vr)])
                else:
                    b2 = (self.wq_pc - 1) % 2
                    ld = self.counts["wl%d" % b2]
                sa = self.I("scalar", lambda e, hw=hw, b2=b2: e.activation(
                    t["wld1"][:, 0:hw], t["wldx"][b2][:, 0:hw], AF.Identity,
                    bias=t["consv"][:, 0:1], scale=t["wred"][:, 4:5]),
                    waits=[("vector", vr), ("wl%d" % b2, ld),
                           ("vector", self.g.get("wq_g2", 0))], inc="scalar")
                g = dict(self.g.get(("wqs", b2), {}))
                g["s"] = sa
                self.g[("wqs", b2)] = g
                g2 = self.I("vector", lambda e, b=b, hw=hw: e.tensor_scalar(
                    t["wq16"][b][:, 0:hw], t["wld1"][:, 0:hw], MAGIC, None,
                    ALU.subtract),
                    waits=[("scalar", sa),
                           ("ws%d" % b, self.g.get(("wqst", b), 0))],
                    inc="vector")
                self.g["wq_g2"] = g2
                st = self.dma("scalar", "ws%d" % b,
                              dst[k * P:(k + 1) * P, h * hw:(h + 1) * hw],
                              t["wq16"][b][:, 0:hw], waits=[("vector", g2)])
                self.g[("wqst", b)] = st
            self.g["wq_act"] = sa

    def ph_stats(self):
        t = self.t
        for i in range(NT_ST):
            b = i % 2
            g = self.g.get(("st", b), {})
            tile = t["xtile"][b]
            ld = self.dma("sync", "xt%d" % b, tile[:],
                          t["in_xst"][i * P:(i + 1) * P, :],
                          waits=[("scalar", g.get("act", 0)),
                                 ("vector", g.get("dve", 0))])
            sa = self.I("scalar", lambda e, tile=tile, i=i: e.activation(
                t["scr_a"][:], tile[:], AF.Square,
                accum_out=t["sqc"][:, i:i + 1]), waits=[("xt%d" % b, ld)],
                inc="scalar")
            self.I("vector", lambda e, tile=tile, i=i: e.tensor_scalar(
                t["scr_v"][:], tile[:], 1.0, 0.0, ALU.mult, ALU.add,
                accum_out=t["sumc"][:, i:i + 1]), waits=[("xt%d" % b, ld)],
                inc="vector")
            v2 = self.I("vector", lambda e, tile=tile, i=i: e.pool_max(
                t["xmaxc"][:, i:i + 1], tile[:]), inc="vector")
            self.g[("st", b)] = {"act": sa, "dve": v2}
        c_act, c_dve = self.counts["scalar"], self.counts["vector"]

        cc = 1.0 / D
        self.I("vector", lambda e: e.tensor_scalar(
            t["m_t"][:], t["sumc"][:], cc, None, ALU.mult),
            waits=[("vector", c_dve), ("scalar", c_act)], inc="vector")
        self.I("vector", lambda e: e.tensor_tensor(
            t["var_t"][:], t["m_t"][:], t["m_t"][:], ALU.mult), inc="vector")
        self.I("vector", lambda e: e.scalar_tensor_tensor(
            t["var_t"][:], t["sqc"][:], cc, t["var_t"][:], ALU.mult,
            ALU.subtract), inc="vector")
        vv = self.I("vector", lambda e: e.tensor_copy(
            t["sd_t"][:], t["var_t"][:]), inc="vector")
        sa = self.I("scalar", lambda e: e.activation(
            t["sd_t"][:], t["sd_t"][:], AF.Sqrt, bias=t["consv"][:, 1:2]),
            waits=[("vector", vv)], inc="scalar")
        self.I("vector", lambda e: e.reciprocal(t["rstd_t"][:], t["sd_t"][:]),
               waits=[("scalar", sa)], inc="vector")
        self.I("vector", lambda e: e.scalar_tensor_tensor(
            t["xbias_t"][:], t["m_t"][:], -1.0, t["rstd_t"][:], ALU.mult,
            ALU.mult), inc="vector")
        self.I("vector", lambda e: e.tensor_tensor(
            t["cand_t"][:], t["xmaxc"][:], t["m_t"][:], ALU.subtract),
            inc="vector")
        self.I("vector", lambda e: e.tensor_tensor(
            t["cand_t"][:], t["cand_t"][:], t["rstd_t"][:], ALU.mult),
            inc="vector")
        vred = self.I("vector", lambda e: e.pool_max(
            t["red1"][:, 0:1], t["cand_t"][:]), inc="vector")
        self.v_stats = self.counts["vector"]
        self.g_cand = self.I("gpsimd", lambda e: e.reduce_max(
            t["scal"][0:1, 0:1], t["red1"][:, 0:1], AX.C),
            waits=[("vector", vred)], inc="gpsimd")

    def ph_s1(self):
        t = self.t
        self.W("gpsimd", [("gpsimd", self.g_cand)])
        self.send(0, t["scal"][0:1, 0:1], 1)
        rd = self.recv(0, t["exch2"][0:2, 0:1], t["shm_pay"][0])
        self.g1_s1 = self.I("gpsimd", lambda e: e.reduce_max(
            t["scal"][0:1, 1:2], t["exch2"][0:2, 0:1], AX.C),
            waits=[("q_gpsimd", rd)], inc="gpsimd")

    def ph_s1b(self):
        t = self.t
        self.I("vector", lambda e: e.tensor_scalar(
            t["scal"][0:1, 2:3], t["scal"][0:1, 1:2], EPS_Q, 1.0 / 255.0,
            ALU.max, ALU.mult), waits=[("gpsimd", self.g1_s1)],
            inc="vector")
        v2 = self.I("vector", lambda e: e.reciprocal(
            t["scal"][0:1, 3:4], t["scal"][0:1, 2:3]), inc="vector")
        mm = self.I("tensor", lambda e: e.matmul(
            t["ps_bc"][0][:, 0:2], t["ones1"][0:1, :], t["scal"][0:1, 2:4],
            start=True, stop=True),
            waits=[("vector", v2), ("bC", self.c_consts)], inc="tensor")
        ve = self.I("vector", lambda e: e.tensor_copy(
            t["s1v"][:, 0:2], t["ps_bc"][0][:, 0:2]),
            waits=[("tensor", mm)], inc="vector")
        self.I("vector", lambda e: e.tensor_scalar(
            t["aq_t"][:], t["rstd_t"][:], t["s1v"][:, 1:2], None, ALU.mult),
            waits=[("vector", ve)], inc="vector")
        self.v_aq = self.I("vector", lambda e: e.tensor_scalar(
            t["bq_t"][:], t["xbias_t"][:], t["s1v"][:, 1:2], MAGIC, ALU.mult,
            ALU.add), inc="vector")
        self.I("vector", lambda e: e.tensor_scalar(
            t["s1sw"][:], t["sw1"][:], t["s1v"][:, 0:1], None, ALU.mult),
            inc="vector")
        self.I("vector", lambda e: e.reciprocal(t["b1q"][:], t["s1sw"][:]),
               inc="vector")
        self.I("vector", lambda e: e.tensor_tensor(
            t["b1q"][:], t["b1t"][:], t["b1q"][:], ALU.mult), inc="vector")
        self.I("vector", lambda e: e.tensor_scalar(
            t["b1q"][:], t["b1q"][:], MAGIC, MAGIC, ALU.add, ALU.subtract),
            inc="vector")
        self.v_b1q = self.I("vector", lambda e: e.tensor_tensor(
            t["b1q"][:], t["b1q"][:], t["s1sw"][:], ALU.mult), inc="vector")

    def _xq_tiles(self, i0, n, dest, act_t, bias_t, relu0, vwait=0,
                  sa_extra=(), ld_extra=()):
        t = self.t
        marks = []
        for j in range(n):
            i = i0 + j
            b = j % 2
            g = self.g.get(("xt", b), {})
            gst = self.g.get(("st", b), {})
            ld = self.dma("sync", "xt%d" % b, t["xtile"][b][:],
                          t["in_xst"][i * P:(i + 1) * P, :],
                          waits=[("scalar", g.get("act", gst.get("act", 0))),
                                 ("vector", g.get("dve", gst.get("dve", 0)))] +
                          list(ld_extra))
            sa = self.I("scalar", lambda e, i=i, b=b, act_t=act_t, bias_t=bias_t:
                        e.activation(t["y32"][b][:], t["xtile"][b][:],
                                     AF.Identity, bias=bias_t[:, i:i + 1],
                                     scale=act_t[:, i:i + 1]),
                        waits=[("xt%d" % b, ld), ("vector", vwait),
                               ("vector", g.get("q", 0))] + list(sa_extra),
                        inc="scalar")
            if relu0:
                vq = self.I("vector", lambda e, b=b: e.tensor_scalar(
                    t["y16"][b][:], t["y32"][b][:], MAGIC, 0.0, ALU.subtract,
                    ALU.max), waits=[("scalar", sa), ("tensor", g.get("tr", 0))],
                    inc="vector")
            else:
                vq = self.I("vector", lambda e, b=b: e.tensor_copy(
                    t["y16"][b][:], t["y32"][b][:]),
                    waits=[("scalar", sa), ("tensor", g.get("tr", 0))],
                    inc="vector")
            self.g[("xt", b)] = {"act": sa, "dve": vq, "q": vq}
            for k in range(DC):
                th = self.tr_ctr % 2
                self.tr_ctr += 1
                mm = self.I("tensor", lambda e, b=b, k=k, th=th: e.transpose(
                    t["ps_tr"][th][:, 0:P],
                    t["y16"][b][:, k * P:(k + 1) * P], t["eye16"][:]),
                    waits=[("vector", vq), ("bC", self.c_consts),
                           ("vector", self.g.get(("tr_ev", th), 0))],
                    inc="tensor")
                if k == DC - 1:
                    self.g[("xt", b)]["tr"] = mm
                ve = self.I("vector", lambda e, dest=dest, j=j, k=k, th=th:
                            e.tensor_copy(dest[k][:, j * P:(j + 1) * P],
                                          t["ps_tr"][th][:, 0:P]),
                    waits=[("tensor", mm)], inc="vector")
                self.g[("tr_ev", th)] = ve
            marks.append(self.counts["vector"])
        return marks

    def ph_attn(self):
        t = self.t
        self.ph_wt()
        xnb_marks = self._xq_tiles(0, NT_OWN, t["xnb"], t["rstd_t"],
                                   t["xbias_t"], relu0=False,
                                   vwait=self.v_stats)
        v_xnb = xnb_marks[-1]
        self.v_xnb = v_xnb
        self.g["wq_act"] = self.counts["scalar"]
        self.g["wq_mx"] = self.counts["vector"]
        self.I("vector", lambda e: e.memset(t["sumh2"][0:1, :], 0.0),
               inc="vector")
        for m in range(DC):
            for tb in range(NTB_OWN):
                kf = [(t["wtf"][k][:, m * P:(m + 1) * P],
                       t["xnb"][k][:, tb * TB:(tb + 1) * TB]) for k in range(DC)]
                bank, mmf = self.mmgroup(kf, waits=[
                    ("vector", xnb_marks[4 * tb + 3]), ("bWt", self.c_wt)])
                sf = self.I("scalar", lambda e, tb=tb, bank=bank: e.activation(
                    t["fbuf"][:, tb * TB:(tb + 1) * TB],
                    t["ps_mm"][bank][:, 0:TB], AF.Sigmoid),
                    waits=[("tensor", mmf),
                           ("vector", self.g.get(("scan2", m - 1), 0))],
                    inc="scalar")
                self.bank_done(bank, "scalar", sf)
                ki = [(t["wti"][k][:, m * P:(m + 1) * P],
                       t["xnb"][k][:, tb * TB:(tb + 1) * TB]) for k in range(DC)]
                bank, mmi = self.mmgroup(ki)
                si = self.I("scalar", lambda e, bank=bank: e.activation(
                    t["sib"][:, 0:TB], t["ps_mm"][bank][:, 0:TB], AF.Silu),
                    waits=[("tensor", mmi),
                           ("vector", self.g.get("inpn", 0))], inc="scalar")
                self.bank_done(bank, "scalar", si)
                self.g["inpn"] = self.I(
                    "vector", lambda e, tb=tb: e.scalar_tensor_tensor(
                        t["ibuf"][:, tb * TB:(tb + 1) * TB],
                        t["fbuf"][:, tb * TB:(tb + 1) * TB], 1.0,
                        t["sib"][:, 0:TB], ALU.subtract, ALU.mult),
                    waits=[("scalar", sf), ("scalar", si)], inc="vector")
                kg = [(t["wtg"][k][:, m * P:(m + 1) * P],
                       t["xnb"][k][:, tb * TB:(tb + 1) * TB]) for k in range(DC)]
                bank, mmg = self.mmgroup(kg)
                b = tb % 2
                sg = self.I("scalar", lambda e, b=b, bank=bank: e.activation(
                    t["gs16"][b][:, 0:TB], t["ps_mm"][bank][:, 0:TB], AF.Silu),
                    waits=[("tensor", mmg),
                           ("sg%d" % b, self.g.get(("gs_st", b), 0))],
                    inc="scalar")
                self.bank_done(bank, "scalar", sg)
                st = self.dma("sync", "sg%d" % b,
                              t["dsc_gs"][m][:, tb * TB:(tb + 1) * TB],
                              t["gs16"][b][:, 0:TB], waits=[("scalar", sg)])
                self.g[("gs_st", b)] = st
                self.g["sg_last"] = sg
            hb = t["hrot"]
            v1 = self.I("vector", lambda e, hb=hb: e.tensor_tensor_scan(
                hb[:], t["fbuf"][:], t["ibuf"][:], 0.0, ALU.mult, ALU.add),
                waits=[("vector", self.g["inpn"]),
                       ("bAh", self.g.get("h_st", 0)),
                       ("q_gpsimd", self.g.get("carry_rd", 0)),
                       ("scalar", self.g.get("h2_act", 0))],
                inc="vector")
            self.W("gpsimd", [("vector", v1),
                              ("vector", self.g.get("init_tt", 0))])
            self.send(1 + m, hb[:, T_OWN - 1:T_OWN], P)
            rd = self.recv(1 + m, t["recv_c"][:, 0:1],
                           t["shm_pay"][1 + m][0:P, :])
            self.g["carry_rd"] = rd
            self.g["init_tt"] = self.I("vector", lambda e: e.tensor_tensor(
                t["init_c"][:, 0:1], t["recv_c"][:, 0:1], t["cmaskv"][:, 0:1],
                ALU.mult), waits=[("q_gpsimd", rd), ("bC", self.c_consts)],
                inc="vector")
            v2 = self.I("vector", lambda e, hb=hb: e.tensor_tensor_scan(
                hb[:], t["fbuf"][:], t["ibuf"][:], t["init_c"][:, 0:1],
                ALU.mult, ALU.add), inc="vector")
            self.g[("scan2", m)] = v2
            for blk in range(NTB_OWN):
                sa = self.I("scalar", lambda e, hb=hb, blk=blk: e.activation(
                    t["h2t"][:, 0:TB], hb[:, blk * TB:(blk + 1) * TB],
                    AF.Square), waits=[("vector", v2),
                                       ("tensor", self.g.get("h2_mm", 0))],
                    inc="scalar")
                self.g["h2_act"] = sa
                mm = self.I("tensor", lambda e: e.matmul(
                    t["ps_row"][0:1, 0:TB], t["onesb"][:, 0:1],
                    t["h2t"][:, 0:TB], start=True, stop=True),
                    waits=[("scalar", sa), ("bC", self.c_consts),
                           ("vector", self.g.get("h2_add", 0))],
                    inc="tensor")
                self.g["h2_mm"] = mm
                self.g["h2_add"] = self.I(
                    "vector", lambda e, blk=blk: e.tensor_tensor(
                        t["sumh2"][0:1, blk * TB:(blk + 1) * TB],
                        t["sumh2"][0:1, blk * TB:(blk + 1) * TB],
                        t["ps_row"][0:1, 0:TB], ALU.add),
                    waits=[("tensor", mm)], inc="vector")
            self.g["h_st"] = self.dma("sync", "bAh", t["dsc_ah"][m], hb[:],
                                      waits=[("vector", v2)])
            self._quant_w(t["in_w1"], t["dsc_wq1n"], t["sw1"],
                          4 * m, 4 * m + 4, D)
            self._quant_w(t["in_w2"], t["dsc_wq2n"], t["sw2"], m, m + 1, F)
        self.g["attn_mm_done"] = self.counts["tensor"]
        self.ph_s1b()
        v = self.I("vector", lambda e: e.tensor_scalar(
            t["sumh2"][0:1, :], t["sumh2"][0:1, :], 1.0 / D, None, ALU.mult),
            waits=[("vector", self.g["h2_add"])], inc="vector")
        sa = self.I("scalar", lambda e: e.activation(
            t["sumh2"][0:1, :], t["sumh2"][0:1, :], AF.Sqrt,
            bias=t["consv"][0:1, 1:2]), waits=[("vector", v)], inc="scalar")
        self.v_rstd2 = self.I("vector", lambda e: e.reciprocal(
            t["sumh2"][0:1, :], t["sumh2"][0:1, :]), waits=[("scalar", sa)],
            inc="vector")

        for m in range(DC):
            for blk in range(NTB_OWN):
                b = (m * NTB_OWN + blk) % 2
                g = self.g.get(("tl", b), {})
                ldh = self.dma("sync", "lh%d" % b, t["hrt"][b][:, 0:TB],
                               t["dsc_ah"][m][:, blk * TB:(blk + 1) * TB],
                               waits=[("vector", g.get("t1", self.v_castw)),
                                      ("bAh", self.g.get("h_st", 0))])
                ldg = self.dma("sync", "lg%d" % b, t["gs16"][b][:, 0:TB],
                               t["dsc_gs"][m][:, blk * TB:(blk + 1) * TB],
                               waits=[("vector", g.get("o", 0)),
                                      ("sg0", self.g.get(("gs_st", 0), 0)),
                                      ("sg1", self.g.get(("gs_st", 1), 0)),
                                      ("scalar", self.g.get("sg_last", 0))])
                pbk = blk % 2
                mmb = self.I("tensor", lambda e, blk=blk, pbk=pbk: e.matmul(
                    t["ps_bc"][pbk][:, 0:TB], t["ones1"][0:1, :],
                    t["sumh2"][0:1, blk * TB:(blk + 1) * TB],
                    start=True, stop=True),
                    waits=[("vector", self.v_rstd2),
                           ("vector", g.get("t1", 0))], inc="tensor")
                v1 = self.I("vector", lambda e, b=b, pbk=pbk: e.tensor_tensor(
                    t["t1t"][b][:, 0:TB], t["hrt"][b][:, 0:TB],
                    t["ps_bc"][pbk][:, 0:TB], ALU.mult),
                    waits=[("tensor", mmb), ("lh%d" % b, ldh),
                           ("scalar", g.get("t2", 0))], inc="vector")
                s2 = self.I("scalar", lambda e, b=b, m=m: e.activation(
                    t["t2t"][b][:, 0:TB], t["t1t"][b][:, 0:TB], AF.Identity,
                    scale=t["negnw"][:, m:m + 1]),
                    waits=[("vector", v1), ("vector", g.get("o", 0))],
                    inc="scalar")
                vo = self.I("vector", lambda e, b=b: e.tensor_tensor(
                    t["o16"][b][:, 0:TB], t["t2t"][b][:, 0:TB],
                    t["gs16"][b][:, 0:TB], ALU.mult),
                    waits=[("scalar", s2), ("lg%d" % b, ldg),
                           ("so%d" % b, g.get("ost", 0)),
                           ("ws0", self.c_cast[0]),
                           ("ws1", self.c_cast[1])], inc="vector")
                ost = self.dma("sync", "so%d" % b,
                               t["dsc_o"][m][:, blk * TB:(blk + 1) * TB],
                               t["o16"][b][:, 0:TB], waits=[("vector", vo)])
                self.g[("tl", b)] = {"t1": v1, "t2": s2, "o": vo, "ost": ost}
        self.c_o_sp = (self.counts["so0"], self.counts["so1"])
        self.v_tail = self.counts["vector"]
        self.g["tail_mm"] = self.counts["tensor"]

    def ph_ffn1(self):
        t = self.t
        for k in range(DC):
            for hh in range(2):
                self.dma("scalar", "bW1r",
                         t["wq1r"][k][:, hh * 2048:(hh + 1) * 2048],
                         t["dsc_wq1n"][hh * 2048:(hh + 1) * 2048,
                                       k * P:(k + 1) * P],
                         waits=[("ws0", self.counts["ws0"]),
                                ("ws1", self.counts["ws1"]),
                                ("tensor", self.g.get("attn_mm_done", 0))],
                         transpose=True)
        c_wq1r = self.counts["bW1r"]
        own_col = 0
        for_col = 0
        vw = max(self.v_aq, self.g.get("wq_g2", 0), self.g.get("wq_mx", 0))
        sa_x = [("wl0", self.counts["wl0"]), ("wl1", self.counts["wl1"])]
        for bx in range(2):
            gx = dict(self.g.get(("xt", bx), {}))
            gq = self.g.get(("wqs", bx), {})
            gx["act"] = max(gx.get("act", 0), gq.get("s", 0),
                            self.g.get("wq_act", 0))
            gx["dve"] = max(gx.get("dve", 0), gq.get("v", 0))
            self.g[("xt", bx)] = gx
        pm = {0: self._xq_tiles(0, 4, t["xqf_pb"][0], t["aq_t"], t["bq_t"],
                                relu0=True, vwait=vw, sa_extra=sa_x,
                                ld_extra=sa_x)}
        for tb in range(NTB_ALL):
            if tb + 1 < NTB_ALL:
                pm[tb + 1] = self._xq_tiles(
                    (tb + 1) * 4, 4, t["xqf_pb"][(tb + 1) % 2],
                    t["aq_t"], t["bq_t"], relu0=True, vwait=vw,
                    sa_extra=sa_x, ld_extra=sa_x)
            own = tb < NTB_OWN
            xq = t["xqf_pb"][tb % 2]
            for fk in range(FC):
                kq = [(t["wq1r"][k][:, fk * P:(fk + 1) * P],
                       xq[k]) for k in range(DC)]
                bank, mm = self.mmgroup(kq, waits=[
                    ("vector", pm[tb][-1]), ("bW1r", c_wq1r)])
                if own:
                    sb = (tb * FC + fk) % 2
                    sa = self.I("scalar", lambda e, fk=fk, bank=bank, sb=sb:
                        e.activation(t["hsp"][sb][:, 0:TB],
                                     t["ps_mm"][bank][:, 0:TB], AF.Relu,
                                     bias=t["b1q"][:, fk:fk + 1],
                                     scale=t["s1sw"][:, fk:fk + 1]),
                        waits=[("tensor", mm), ("vector", self.v_b1q),
                               ("st%d" % sb, self.g.get(("hsp_st", sb), 0)),
                               ("vector", self.g.get(("hsp_mx", sb), 0)),
                               ("ws0", self.counts["ws0"]),
                               ("ws1", self.counts["ws1"])],
                        inc="scalar")
                    self.bank_done(bank, "scalar", sa)
                    st = self.dma("sync", "st%d" % sb,
                                  t["dsc_h"][fk][:, tb * TB:(tb + 1) * TB],
                                  t["hsp"][sb][:, 0:TB],
                                  waits=[("scalar", sa)])
                    self.g[("hsp_st", sb)] = st
                    vm = self.I("vector", lambda e, sb=sb, c=own_col:
                        e.pool_max(t["hmaxs"][:, c:c + 1],
                                   t["hsp"][sb][:, 0:TB]),
                        waits=[("scalar", sa)], inc="vector")
                    self.g[("hsp_mx", sb)] = vm
                    own_col += 1
                else:
                    vm = self.I("vector", lambda e, bank=bank, c=for_col:
                        e.pool_max(t["imaxs"][:, c:c + 1],
                                   t["ps_mm"][bank][:, 0:TB]),
                        waits=[("tensor", mm)], inc="vector")
                    self.bank_done(bank, "vector", vm)
                    for_col += 1
        self.g["mm1_half"] = self.counts["tensor"]
        self.n_own = own_col
        self.n_for = for_col
        self.v_ffn1 = self.counts["vector"]

    def ph_s2(self):
        t = self.t
        imaxs_h = t["imaxs_h"]
        for fk in range(FC):
            self.I("vector", lambda e, fk=fk: e.pool_max(
                t["imaxf"][:, fk:fk + 1],
                bass.AP(imaxs_h, fk, [[imaxs_h.shape[1], P], [FC, 12]])),
                waits=[("vector", self.v_ffn1)], inc="vector")
        self.I("vector", lambda e: e.tensor_tensor(
            t["imaxf"][:], t["imaxf"][:], t["s1sw"][:], ALU.mult), inc="vector")
        self.I("vector", lambda e: e.tensor_tensor(
            t["imaxf"][:], t["imaxf"][:], t["b1q"][:], ALU.add), inc="vector")
        self.I("vector", lambda e: e.tensor_scalar(
            t["imaxf"][:], t["imaxf"][:], 0.0, None, ALU.max), inc="vector")
        self.I("vector", lambda e: e.pool_max(
            t["red1"][:, 1:2], t["imaxf"][:]), inc="vector")
        self.I("vector", lambda e: e.pool_max(
            t["red1"][:, 2:3], t["hmaxs"][:, 0:self.n_own]),
            waits=[("vector", self.v_ffn1)], inc="vector")
        v3 = self.I("vector", lambda e: e.pool_max(
            t["red1"][:, 3:4], t["red1"][:, 1:3]), inc="vector")
        g1 = self.I("gpsimd", lambda e: e.reduce_max(
            t["scal"][0:1, 4:5], t["red1"][:, 3:4], AX.C),
            waits=[("vector", v3)], inc="gpsimd")
        self.W("gpsimd", [("gpsimd", g1)])
        self.send(9, t["scal"][0:1, 4:5], 1)
        rd = self.recv(9, t["exch2"][0:2, 0:1], t["shm_pay"][9])
        g2 = self.I("gpsimd", lambda e: e.reduce_max(
            t["scal"][0:1, 5:6], t["exch2"][0:2, 0:1], AX.C),
            waits=[("q_gpsimd", rd)], inc="gpsimd")
        self.I("vector", lambda e: e.tensor_scalar(
            t["scal"][0:1, 6:7], t["scal"][0:1, 5:6], EPS_Q, 1.0 / 255.0,
            ALU.max, ALU.mult), waits=[("gpsimd", g2)], inc="vector")
        v4 = self.I("vector", lambda e: e.reciprocal(
            t["scal"][0:1, 7:8], t["scal"][0:1, 6:7]), inc="vector")
        mm = self.I("tensor", lambda e: e.matmul(
            t["ps_bc"][0][:, 2:4], t["ones1"][0:1, :], t["scal"][0:1, 6:8],
            start=True, stop=True), waits=[("vector", v4)], inc="tensor")
        ve = self.I("vector", lambda e: e.tensor_copy(
            t["s1v"][:, 2:4], t["ps_bc"][0][:, 2:4]),
            waits=[("tensor", mm)], inc="vector")
        self.v_s2v = ve

        mmt = self.I("tensor", lambda e: e.transpose(
            t["ps_bc"][0][0:DC, 0:P], t["sw2"][:], t["eye32"][:]),
            waits=[("vector", ve)], inc="tensor")
        vev = self.I("vector", lambda e: e.tensor_copy(
            t["sw2r8"][0:DC, :], t["ps_bc"][0][0:DC, 0:P]),
            waits=[("tensor", mmt)], inc="vector")
        st = self.dma("sync", "bRow", t["dsc_row"], t["sw2r8"][0:DC, :],
                      waits=[("vector", vev)])
        ld = self.dma("sync", "bRow", t["sw2row"][0:1, :],
                      t["dsc_row"].rearrange("a b -> (a b)")[None, :],
                      waits=[("bRow", st),
                             ("tensor", self.g.get("tail_mm", 0)),
                             ("vector", self.v_tail)])
        v5 = self.I("vector", lambda e: e.tensor_scalar(
            t["sw2row"][0:1, :], t["sw2row"][0:1, :], t["scal"][0:1, 6:7],
            None, ALU.mult), waits=[("bRow", ld), ("vector", ve)],
            inc="vector")
        self.I("vector", lambda e: e.reciprocal(
            t["b2qrow"][0:1, :], t["sw2row"][0:1, :]), inc="vector")
        self.I("vector", lambda e: e.tensor_tensor(
            t["b2qrow"][0:1, :], t["b2row"][0:1, :], t["b2qrow"][0:1, :],
            ALU.mult), waits=[("bC", self.c_consts)], inc="vector")
        self.I("vector", lambda e: e.tensor_scalar(
            t["b2qrow"][0:1, :], t["b2qrow"][0:1, :], MAGIC, MAGIC, ALU.add,
            ALU.subtract), inc="vector")
        v6 = self.I("vector", lambda e: e.tensor_tensor(
            t["b2qrow"][0:1, :], t["b2qrow"][0:1, :], t["sw2row"][0:1, :],
            ALU.mult), inc="vector")
        prev = v6
        first = True
        for (row, dstt) in ((t["sw2row"], t["s2swb"]), (t["b2qrow"], t["b2qb"])):
            for hh in range(2):
                mmb = self.I("tensor", lambda e, row=row, hh=hh: e.matmul(
                    t["ps_bc"][1][:, 0:TB], t["ones1"][0:1, :],
                    row[0:1, hh * TB:(hh + 1) * TB], start=True, stop=True),
                    waits=[("vector", prev), ("vector", self.v_tail)],
                    inc="tensor")
                w = [("tensor", mmb)]
                if first:
                    w += [("so0", self.c_o_sp[0]), ("so1", self.c_o_sp[1]),
                          ("bAh", self.counts["bAh"]), ("vector", self.v_tail)]
                    first = False
                prev = self.I("vector", lambda e, dstt=dstt, hh=hh: e.tensor_copy(
                    dstt[:, hh * TB:(hh + 1) * TB], t["ps_bc"][1][:, 0:TB]),
                    waits=w, inc="vector")
        self.v_bc2 = prev

    def ph_final(self):
        t = self.t
        for fk in range(FC):
            self.dma("scalar", "bW2r", t["wq2r"][fk][:],
                     t["dsc_wq2n"][0:D, fk * P:(fk + 1) * P],
                     waits=[("ws0", self.counts["ws0"]),
                            ("ws1", self.counts["ws1"]),
                            ("tensor", self.g.get("mm1_half", 0))],
                     transpose=True)
        c_wq2r = self.counts["bW2r"]
        for k in range(DC):
            self.dma("scalar", "bWo", t["wto"][k][:],
                     t["dsc_wo"][0:D, k * P:(k + 1) * P],
                     waits=[("ws0", self.c_cast[0]),
                            ("ws1", self.c_cast[1])], transpose=True)
        c_wto = self.counts["bWo"]

        pool = t["hqpool"]
        npool = len(pool)
        dtot = [(ch, self.counts[ch]) for ch in
                ("st0", "st1", "so0", "so1", "lg0", "lg1")]
        E = []
        base_war = self.g["mm1_half"]
        for tb in range(NTB_OWN):
            late = {}
            ts2m = {}
            for fk in range(FC):
                slot = fk % 4
                b = fk % 2
                idx = tb * FC + fk
                pi, idx2 = idx % npool, idx - npool
                war = base_war if idx2 < 0 else E[idx2 // FC]
                late[fk] = (idx2 >= 0 and idx2 // FC == tb - 1)
                gs = self.g.get(("hqs", slot), None)
                if gs is None:
                    ldw = [("scalar",
                            max(self.g.get(("xt", 0), {}).get("act", 0),
                                self.g.get(("xt", 1), {}).get("act", 0))),
                           ("vector", self.v_ffn1)]
                else:
                    ldw = [("scalar", gs)]
                ld = self.dma("sync", "xh%d" % slot, t["hqld"][slot],
                              t["dsc_h"][fk][:, tb * TB:(tb + 1) * TB],
                              waits=ldw + [("st0", self.counts["st0"]),
                                           ("st1", self.counts["st1"])])
                sa = self.I("scalar", lambda e, slot=slot, b=b: e.activation(
                    t["hqy"][b], t["hqld"][slot], AF.Identity,
                    bias=t["consv"][:, 0:1], scale=t["s1v"][:, 3:4]),
                    waits=[("xh%d" % slot, ld), ("vector", self.v_s2v),
                           ("vector", self.g.get(("hqyw", b), 0)),
                           ("st0", self.counts["st0"]),
                           ("st1", self.counts["st1"])], inc="scalar")
                self.g[("hqs", slot)] = sa
                v2 = self.I("vector", lambda e, pi=pi, b=b: e.tensor_scalar(
                    pool[pi][:, 0:TB], t["hqy"][b], MAGIC, None, ALU.subtract),
                    waits=[("scalar", sa), ("tensor", war)] + dtot,
                    inc="vector")
                self.g[("hqyw", b)] = v2
                ts2m[fk] = v2
            fk_order = sorted(range(FC), key=lambda fk: late[fk])
            for k in range(DC):
                self.dma("sync", "bOf", t["of"][k][:, 0:TB],
                         t["dsc_o"][k][:, tb * TB:(tb + 1) * TB],
                         waits=[("so0", self.c_o_sp[0]), ("so1", self.c_o_sp[1]),
                                ("tensor", self.g.get("mm2", 0)),
                                ("vector", self.v_tail)])
            c_of = self.counts["bOf"]
            for tc in range(4):
                for dh in range(2):
                    i = tb * 4 + tc
                    self.dma("sync", "bXre", t["xre"][tc * 2 + dh],
                             t["in_xst"][i * P:(i + 1) * P,
                                         dh * TB:(dh + 1) * TB],
                             waits=[("vector", self.g.get("fin", self.v_ffn1)),
                                    ("vector", self.v_tail),
                                    ("scalar", self.g.get(("xt", 0), {}).get("act", 0)),
                                    ("scalar", self.g.get(("xt", 1), {}).get("act", 0))])
            c_xre = self.counts["bXre"]
            for tc in range(4):
                tcg = tb * 4 + tc
                for dh in range(2):
                    kb = [(pool[(tb * FC + fk) % npool][:, tc * P:(tc + 1) * P],
                           t["wq2r"][fk][:, dh * TB:(dh + 1) * TB])
                          for fk in fk_order]
                    bankb, mmb = self.mmgroup(kb, waits=[
                        ("bW2r", c_wq2r)],
                        mm_waits=[[("vector", ts2m[fk])] for fk in fk_order])
                    ka = [(t["of"][k][:, tc * P:(tc + 1) * P],
                           t["wto"][k][:, dh * TB:(dh + 1) * TB])
                          for k in range(DC)]
                    banka, mma = self.mmgroup(ka, waits=[
                        ("bOf", c_of), ("bWo", c_wto)])
                    fb = (tc * 2 + dh) % 2
                    gf = self.g.get(("fin", fb), {})
                    v1 = self.I("vector", lambda e, fb=fb, dh=dh, bankb=bankb:
                        e.tensor_tensor(t["finu"][fb][:, 0:TB],
                                        t["ps_mm"][bankb][:, 0:TB],
                                        t["s2swb"][:, dh * TB:(dh + 1) * TB],
                                        ALU.mult),
                        waits=[("tensor", mmb), ("ou%d" % fb, gf.get("st", 0))],
                        inc="vector")
                    self.I("vector", lambda e, fb=fb, dh=dh: e.tensor_tensor(
                        t["finu"][fb][:, 0:TB], t["finu"][fb][:, 0:TB],
                        t["b2qb"][:, dh * TB:(dh + 1) * TB], ALU.add),
                        inc="vector")
                    v3 = self.I("vector", lambda e, fb=fb, banka=banka:
                        e.tensor_tensor(t["finu"][fb][:, 0:TB],
                                        t["finu"][fb][:, 0:TB],
                                        t["ps_mm"][banka][:, 0:TB], ALU.add),
                        waits=[("tensor", mma)], inc="vector")
                    self.bank_done(banka, "vector", v3)
                    self.bank_done(bankb, "vector", v1)
                    v4 = self.I("vector", lambda e, fb=fb, tc=tc, dh=dh:
                        e.tensor_tensor(t["outst"][fb][:, 0:TB],
                                        t["finu"][fb][:, 0:TB],
                                        t["xre"][tc * 2 + dh][:, 0:TB],
                                        ALU.add),
                        waits=[("bXre", c_xre)], inc="vector")
                    self.g["fin"] = v4
                    st = self.dma("sync", "ou%d" % fb,
                                  t["out"][tcg * P:(tcg + 1) * P,
                                           dh * TB:(dh + 1) * TB],
                                  t["outst"][fb][:, 0:TB],
                                  waits=[("vector", v4)])
                    self.g[("fin", fb)] = {"st": st}
            self.g["mm2"] = self.counts["tensor"]
            E.append(self.counts["tensor"])


def build_program(taps=()):
    nc = bass.Bass("TRN2", target_bir_lowering=False)
    t = {}

    def dram(name, shape, dtype=F32, **kw):
        t[name] = nc.dram_tensor(name, shape, dtype, **kw).ap()

    dram("in_xst", [T_OWN + T_FOR, D], kind="ExternalInput")
    for w in ("wi", "wf", "wg", "wo"):
        dram("in_" + w, [D, D], kind="ExternalInput")
    dram("in_w1", [F, D], kind="ExternalInput")
    dram("in_w2", [D, F], kind="ExternalInput")
    dram("in_b1t", [P, FC], kind="ExternalInput")
    dram("in_b2row", [1, D], kind="ExternalInput")
    dram("in_nwt", [P, DC], kind="ExternalInput")
    dram("in_eye16", [P, P], BF16, kind="ExternalInput")
    dram("in_eye32", [P, P], kind="ExternalInput")
    dram("in_ones1", [1, P], kind="ExternalInput")
    dram("in_onesb", [P, 1], BF16, kind="ExternalInput")
    dram("in_cmaskv", [P, 1], kind="ExternalInput")
    dram("in_consv", [P, 2], kind="ExternalInput")
    dram("in_nonces_i", [1, 2 * NEX], I32, kind="ExternalInput")
    dram("in_parity_i", [1, 1], I32, kind="ExternalInput")
    dram("out", [T_OWN, D], kind="ExternalOutput")
    for w in ("wi", "wf", "wg", "wo"):
        dram("dsc_" + w, [D, D], BF16)
    dram("dsc_wq1n", [F, D], BF16)
    dram("dsc_wq2n", [D, F], BF16)
    dram("dsc_row", [DC, P])
    t["dsc_h"] = [nc.dram_tensor("dsc_h%d" % fk, [P, T_OWN], F32).ap()
                  for fk in range(FC)]
    t["dsc_ah"] = [nc.dram_tensor("dsc_ah%d" % m, [P, T_OWN], F32).ap()
                   for m in range(DC)]
    t["dsc_gs"] = [nc.dram_tensor("dsc_gs%d" % m, [P, T_OWN], BF16).ap()
                   for m in range(DC)]
    t["dsc_o"] = [nc.dram_tensor("dsc_o%d" % m, [P, T_OWN], BF16).ap()
                  for m in range(DC)]
    t["shm_pay"] = []
    for ex in range(NEX):
        rows = 1 if ex in (0, 9) else P
        t["shm_pay"].append(nc.dram_tensor("shm_pay%d" % ex, [2 * rows, 1],
                                           F32, addr_space="Shared").ap())
    t["shm_flag"] = nc.dram_tensor("shm_flag", [1, 2 * NEX], I32,
                                   addr_space="Shared").ap()

    ctx = ExitStack()

    def sb(name, shape, dtype=F32):
        t[name] = ctx.enter_context(nc.sbuf_tensor(name, shape, dtype))
        return t[name]

    def sblist(name, n, shape, dtype=F32):
        t[name] = [ctx.enter_context(nc.sbuf_tensor("%s%d" % (name, i), shape,
                                                    dtype)) for i in range(n)]

    sb("eye16", [P, P], BF16)
    sb("eye32", [P, P])
    sb("ones1", [1, P]); sb("onesb", [P, 1], BF16)
    sb("b1t", [P, FC]); sb("b2row", [1, D]); sb("nwt", [P, DC])
    sb("negnw", [P, DC]); sb("cmaskv", [P, 1]); sb("consv", [P, 2])
    sb("nonces_i", [1, 2 * NEX], I32); sb("parity_i", [1, 1], I32)
    sb("pollf", [1, 2], I32)
    sb("scal", [1, 16]); sb("exch2", [2, 1]); sb("recv_c", [P, 1])
    sb("init_c", [P, 1]); sb("red1", [P, 4])
    for n in ("sumc", "sqc", "xmaxc", "m_t", "var_t", "sd_t", "rstd_t",
              "xbias_t", "cand_t", "aq_t", "bq_t"):
        sb(n, [P, NT_ST])
    sb("sw1", [P, FC]); sb("sw2", [P, DC]); sb("s1sw", [P, FC])
    sb("b1q", [P, FC]); sb("s1v", [P, 4]); sb("wred", [P, 6])
    sb("sumh2", [1, T_OWN])
    sb("sw2r8", [DC, P])
    t["sw2row"] = t["sumh2"][0:1, 0:D]
    t["b2qrow"] = t["sumh2"][0:1, D:2 * D]
    sb("imaxf", [P, FC])
    sb("hmaxs", [P, 128])
    t["imaxs_h"] = sb("imaxs", [P, 12 * FCH * 2])
    sblist("xtile", 2, [P, D])
    sblist("y16", 2, [P, D], BF16)
    sb("scr_a", [P, D]); sb("scr_v", [P, D])
    t["y32"] = [t["scr_a"][:, 0:D], t["scr_v"][:, 0:D]]
    sblist("xnb", DC, [P, T_OWN], BF16)
    finbig = sb("finbig", [P, 4096])
    t["fbuf"] = finbig[:, 0:2048]
    t["ibuf"] = finbig[:, 2048:4096]
    fin_addr = nc.lookup_mloc(finbig).addr
    t["of"] = [nc.alloc_sbuf_tensor_at("of%d" % k, [P, TB], BF16,
                                       offset=fin_addr + k * TB * 2)
               for k in range(DC)]
    t["finu"] = [nc.alloc_sbuf_tensor_at("finu%d" % i, [P, TB], F32,
                                         offset=fin_addr + 8192 + i * TB * 4)
                 for i in range(2)]
    t["outst"] = [nc.alloc_sbuf_tensor_at("outst%d" % i, [P, TB], F32,
                                          offset=fin_addr + 12288 + i * TB * 4)
                  for i in range(2)]
    t["fbuf"] = t["fbuf"]
    sb("sib", [P, TB])
    hrot = sb("hrot", [P, T_OWN])
    t["s2swb"] = hrot[:, 0:D]
    t["b2qb"] = hrot[:, D:2 * D]
    t["hrot"] = hrot
    sb("h2t", [P, TB], BF16)
    sblist("gs16", 2, [P, TB], BF16)
    sblist("hrt", 2, [P, TB]); sblist("t1t", 2, [P, TB]); sblist("t2t", 2, [P, TB])
    sblist("o16", 2, [P, TB], BF16)
    sblist("xqf", DC, [P, TB], BF16)
    sblist("hsp", 2, [P, TB])
    t["hqld"] = [t["scr_a"][:, 0:TB], t["scr_a"][:, TB:2 * TB],
                 t["scr_v"][:, 0:TB], t["scr_v"][:, TB:2 * TB]]
    t["hqy"] = [t["t2t"][0][:, 0:TB], t["t2t"][1][:, 0:TB]]
    pool = [t["xnb"][fk % DC][:, (fk // DC) * TB:(fk // DC + 1) * TB]
            for fk in range(FC)]
    pool += [t["xqf"][k][:, 0:TB] for k in range(DC)]
    for nm in ("hsp0", "hsp1", "sib"):
        src = t["hsp"][0] if nm == "hsp0" else (
            t["hsp"][1] if nm == "hsp1" else t["sib"])
        ad = nc.lookup_mloc(src).addr
        for hh in range(2):
            pool.append(nc.alloc_sbuf_tensor_at(
                "hqp_%s_%d" % (nm, hh), [P, TB], BF16, offset=ad + hh * TB * 2))
    t["wldx"] = [t["xtile"][0], t["xtile"][1]]
    t["wld1"] = t["scr_v"]
    t["wq16"] = [nc.alloc_sbuf_tensor_at(
        "wq16v%d" % b, [P, D], BF16,
        offset=nc.lookup_mloc(t["hsp"][b]).addr) for b in range(2)]
    pool += [t["gs16"][0][:, 0:TB], t["gs16"][1][:, 0:TB],
             t["o16"][0][:, 0:TB], t["o16"][1][:, 0:TB], t["h2t"][:, 0:TB]]
    t["hqpool"] = pool
    t["xre"] = [t["xtile"][0][:, 0:TB], t["xtile"][0][:, TB:2 * TB],
                t["xtile"][1][:, 0:TB], t["xtile"][1][:, TB:2 * TB],
                t["hrt"][0][:, 0:TB], t["hrt"][1][:, 0:TB],
                t["t1t"][0][:, 0:TB], t["t1t"][1][:, 0:TB]]
    wbig = sb("wbig", [P, 20480])
    wb = nc.lookup_mloc(wbig).addr

    def at(name, shape, dtype, off):
        return nc.alloc_sbuf_tensor_at(name, shape, dtype, offset=wb + off)

    t["wti"] = [at("wti%d" % k, [P, D], BF16, k * 2048) for k in range(DC)]
    t["wtf"] = [at("wtf%d" % k, [P, D], BF16, 16384 + k * 2048) for k in range(DC)]
    t["wtg"] = [at("wtg%d" % k, [P, D], BF16, 32768 + k * 2048) for k in range(DC)]
    t["wq1r"] = [at("wq1r%d" % k, [P, F], BF16, k * 8192) for k in range(DC)]
    t["wq2r"] = [at("wq2r%d" % fk, [P, D], BF16, fk * 2048) for fk in range(FC)]
    t["wto"] = [at("wto%d" % k, [P, D], BF16, 65536 + k * 2048) for k in range(DC)]
    t["xqf_pb"] = [[t["xnb"][k][:, b * TB:(b + 1) * TB] for k in range(DC)]
                   for b in range(2)]

    t["ps_mm"] = [ctx.enter_context(nc.psum_tensor("psmm%d" % i, [P, TB], F32))
                  for i in range(4)]
    t["ps_bc"] = [ctx.enter_context(nc.psum_tensor("psbc%d" % i, [P, TB], F32))
                  for i in range(2)]
    t["ps_tr"] = [ctx.enter_context(nc.psum_tensor("pstr%d" % i, [P, P], BF16))
                  for i in range(2)]
    t["ps_row"] = t["ps_bc"][1]

    sems = {}
    for name in ("q_gpsimd", "q_poll", "bC", "bCast", "bWt", "bW1r", "bW2r",
                 "bWo", "bAh", "bRow", "bOf", "bXre",
                 "wl0", "wl1", "ws0", "ws1", "xt0", "xt1",
                 "xh0", "xh1", "xh2", "xh3",
                 "st0", "st1", "sg0", "sg1", "so0", "so1", "lh0", "lh1",
                 "lg0", "lg1", "ou0", "ou1",
                 "tensor", "vector", "scalar", "gpsimd"):
        sems[name] = ctx.enter_context(nc.semaphore(name))

    prog = Prog(nc, t, sems, taps)
    with nc.Block() as block:
        @block.sync
        def _(eng):
            prog.walk("sync", eng)

        @block.scalar
        def _(eng):
            prog.walk("scalar", eng)

        @block.vector
        def _(eng):
            prog.walk("vector", eng)

        @block.tensor
        def _(eng):
            prog.walk("tensor", eng)

        @block.gpsimd
        def _(eng):
            prog.walk("gpsimd", eng)
    ctx.close()
    return nc, t


def host_inputs(x, Wi, Wf, Wg, Wo, nw, w1, b1, w2, b2):
    xf = np.ascontiguousarray(np.asarray(x, np.float32).reshape(NCORE * T_OWN, D))
    eye = np.eye(P).astype(ml_dtypes.bfloat16)
    eye32 = np.eye(P).astype(np.float32)
    ones1 = np.ones((1, P), np.float32)
    onesb = np.ones((P, 1)).astype(ml_dtypes.bfloat16)
    b1t = np.ascontiguousarray(np.asarray(b1, np.float32).reshape(FC, P).T)
    b2row = np.ascontiguousarray(np.asarray(b2, np.float32).reshape(1, D))
    nwt = np.ascontiguousarray(np.asarray(nw, np.float32).reshape(DC, P).T)
    rng = np.random.default_rng()
    consv = np.tile(np.array([[MAGIC, EPS_LN]], np.float32), (P, 1))
    nonces = rng.integers(1, 2 ** 30, size=(1, 2 * NEX)).astype(np.int32)
    Wi, Wf, Wg, Wo = [np.ascontiguousarray(np.asarray(a, np.float32))
                      for a in (Wi, Wf, Wg, Wo)]
    w1 = np.ascontiguousarray(np.asarray(w1, np.float32))
    w2 = np.ascontiguousarray(np.asarray(w2, np.float32))
    in_maps = []
    for c in range(NCORE):
        own = np.arange(c * T_OWN, (c + 1) * T_OWN)
        pair = c // 2
        forall = np.concatenate([np.arange(0, pair * 2 * T_OWN),
                                 np.arange((pair + 1) * 2 * T_OWN,
                                           NCORE * T_OWN)])
        fsl = forall[(c % 2) * T_FOR:(c % 2 + 1) * T_FOR]
        xst = np.ascontiguousarray(np.concatenate([xf[own], xf[fsl]], axis=0))
        in_maps.append(dict(
            in_xst=xst, in_wi=Wi, in_wf=Wf, in_wg=Wg, in_wo=Wo,
            in_w1=w1, in_w2=w2, in_b1t=b1t, in_b2row=b2row, in_nwt=nwt,
            in_eye16=eye, in_eye32=eye32, in_ones1=ones1, in_onesb=onesb,
            in_cmaskv=np.full((P, 1), float(c % 2), np.float32),
            in_consv=consv,
            in_nonces_i=nonces,
            in_parity_i=np.array([[c % 2]], np.int32),
        ))
    return in_maps


def kernel(L0_attn_x, L0_attn_Wi, L0_attn_Wf, L0_attn_Wg, L0_attn_Wo,
           L0_attn_norm_w, L0_ffn_w1, L0_ffn_b1, L0_ffn_w2, L0_ffn_b2):
    in_maps = host_inputs(L0_attn_x, L0_attn_Wi, L0_attn_Wf, L0_attn_Wg,
                          L0_attn_Wo, L0_attn_norm_w, L0_ffn_w1, L0_ffn_b1,
                          L0_ffn_w2, L0_ffn_b2)
    nc, _ = build_program()
    res = run_bass_kernel_spmd(nc, in_maps, core_ids=list(range(NCORE)))
    out = np.concatenate([np.asarray(res.results[c]["out"])
                          for c in range(NCORE)], axis=0)
    return out.reshape(4, 4096, D)
